# version 09
# baseline (speedup 1.0000x reference)
import os
from contextlib import ExitStack

import numpy as np
import ml_dtypes

import concourse.bass as bass
import concourse.mybir as mybir
from concourse.bass_utils import run_bass_kernel_spmd

F32 = mybir.dt.float32
BF16 = mybir.dt.bfloat16
AF = mybir.ActivationFunctionType
ALU = mybir.AluOpType
AX = mybir.AxisListType.X
BF16NP = ml_dtypes.bfloat16

D = 1024
NT = 16
NH = 16
NLH = 8
NE = 8
DBG = os.environ.get("KDBG", "") == "1"
KSTAGE = int(os.environ.get("KSTAGE", "3"))
KTC = int(os.environ.get("KTC", "16"))
KPART = int(os.environ.get("KPART", "9"))
KLAT = int(os.environ.get("KLAT", "9"))

ENGS = ("t", "v", "s", "g", "y")


class P:

    def __init__(self, nc, es):
        self.nc = nc
        self.es = es
        self.ops = {e: [] for e in ENGS}
        self.sem = {}
        self.cnt = {}
        self.hw = {e: {} for e in ENGS}
        for e in ENGS:
            self.new_sem("p" + e)

    def new_sem(self, name):
        self.sem[name] = self.es.enter_context(self.nc.semaphore(name))
        self.cnt[name] = 0
        return name

    def wait(self, eng, tok):
        if tok is None:
            return
        name, val = tok
        if self.hw[eng].get(name, 0) >= val:
            return
        self.hw[eng][name] = val
        h = self.sem[name]
        self.ops[eng].append(lambda e, h=h, v=val: e.wait_ge(h, v))

    def op(self, eng, fn, deps=(), mark=False):
        for t in deps:
            self.wait(eng, t)
        if mark:
            name = "p" + eng
            self.cnt[name] += 1
            tok = (name, self.cnt[name])
            h = self.sem[name]
            self.ops[eng].append(lambda e, fn=fn, h=h: fn(e).then_inc(h, 1))
            return tok
        self.ops[eng].append(fn)
        return None

    def mark(self, eng):
        name = "p" + eng
        self.cnt[name] += 1
        tok = (name, self.cnt[name])
        h = self.sem[name]
        lst = self.ops[eng]
        assert lst, "mark with no prior op"
        prev = lst.pop()
        def wrapped(e, prev=prev, h=h):
            ins = prev(e)
            ins.then_inc(h, 1)
            return ins
        lst.append(wrapped)
        return tok

    def dma(self, eng, out_ap, in_ap, sem_name, deps=(), **kw):
        for t in deps:
            self.wait(eng, t)
        self.cnt[sem_name] += 16
        tok = (sem_name, self.cnt[sem_name])
        h = self.sem[sem_name]
        self.ops[eng].append(
            lambda e, o=out_ap, i=in_ap, h=h, kw=kw: e.dma_start(o, i, **kw).then_inc(h, 16))
        return tok

    def realize(self, block):
        def body(lst):
            return lambda e: [fn(e) for fn in lst]
        block.tensor(body(self.ops["t"]))
        block.vector(body(self.ops["v"]))
        block.scalar(body(self.ops["s"]))
        block.gpsimd(body(self.ops["g"]))
        block.sync(body(self.ops["y"]))


def build_program():
    nc = bass.Bass("TRN2", target_bir_lowering=False)
    es = ExitStack()

    def din(name, shape, dt=F32):
        return nc.dram_tensor(name, shape, dt, kind="ExternalInput").ap()

    x_comb = din("x_comb", [34, 128, D])
    maskT_in = din("maskT", [128, 4 * 512], BF16)
    latmask_in = din("latmask", [128, NT * 64], BF16)
    ident_in = din("ident", [128, 128])
    poolmat_in = din("poolmat", [128, 2])
    glw_in = din("glw", [NH, 64])
    grw_in = din("grw", [NH, 64])
    lrw_in = din("lrw", [8, 128, 12])
    lrb_in = din("lrb", [1, 12])
    wq_in = din("wq", [128, 8192])
    wk_in = din("wk", [128, 8192])
    wv_in = din("wv", [128, 8192])
    wo_in = din("wo", [128, 8192])
    wgq_in = din("wgq", [128, 4096])
    wgk_in = din("wgk", [128, 4096])
    wgv_in = din("wgv", [128, 4096])
    ro_in = din("ro", [64, 8192])
    b1_in = din("b1", [NE, 128, 32])
    b2_in = din("b2", [NE, D])
    w1_in = din("w1", [NE, 32, 128, 1024])
    w2_in = din("w2", [NE, 4, 128, 8192])

    out_ap = nc.dram_tensor("out", [2048, D], F32, kind="ExternalOutput").ap()
    x1s = nc.dram_tensor("x1s", [NT, 128, D], F32).ap()
    x1Ts = nc.dram_tensor("x1Ts", [8, 128, 2048], BF16).ap()

    dbg = {}

    with es:
        p = P(nc, es)

        def sb(name, shape, dt):
            return es.enter_context(nc.sbuf_tensor(name, shape, dt))

        def dump(name, ap, shape, dt, deps):
            if not DBG:
                return None
            t = nc.dram_tensor("dbg_" + name, shape, dt, kind="ExternalOutput").ap()
            sm = p.new_sem("dbg_" + name)
            dbg[name] = p.dma("y", t, ap, sm, deps=deps)
            return dbg[name]

        id_f = sb("id_f", [128, 128], F32)
        id_b = sb("id_b", [128, 128], BF16)
        ones_f = sb("ones_f", [1, 128], F32)
        poolmat = sb("poolmat_s", [128, 2], F32)
        Wg = sb("Wg", [128, 8 * 32], BF16)
        c_sb = sb("c_sb", [128, NT * 8], F32)
        cT = sb("cT", [8, NT * 128], BF16)
        lrw = sb("lrw_s", [128, 8 * 12], F32)
        lrb = sb("lrb_s", [1, 12], F32)
        stat = sb("stat", [128, 64], F32)
        lp_pad = sb("lp_pad", [128, 5], F32)
        rl_sb = sb("rl_sb", [128, 8], F32)
        p8_sb = sb("p8_sb", [128, 8], F32)
        pm_sb = sb("pm_sb", [128, 8], F32)
        ge_sb = sb("ge_sb", [128, 8], F32)

        PB = [es.enter_context(nc.psum_tensor(f"pb{i}", [128, 512], F32)) for i in range(8)]

        p.new_sem("csem")
        p.dma("y", id_f[:, :], ident_in, "csem")
        p.dma("y", poolmat[:, :], poolmat_in, "csem")
        p.dma("y", lrw[:, :].rearrange("p (k c) -> p k c", k=8),
              lrw_in.rearrange("k p c -> p k c"), "csem")
        p.dma("y", lrb[:, :], lrb_in, "csem")
        CONST = ("csem", p.cnt["csem"])
        p.new_sem("gs_idb")
        IDB = p.dma("g", id_b[:, :], ident_in, "gs_idb")

        eps_c = sb("eps_c", [128, 1], F32)
        EPSC = p.op("v", lambda e: e.memset(eps_c[:, :], 1e-5), mark=True)
        t_wg0 = p.op("v", lambda e: e.memset(Wg[:, :], 0.0), mark=True)
        ONESF = p.op("v", lambda e: e.memset(ones_f[:, :], 1.0), mark=True)
        LPP0 = p.op("v", lambda e: e.memset(lp_pad[:, 4:5], 0.0), mark=True)
        p.new_sem("wgsem")
        for h in range(NH):
            dc, po = h // 2, 64 * (h % 2)
            for gi, src in ((0, glw_in), (1, grw_in)):
                p.dma("g", Wg[po:po + 64, dc * 32 + 16 * gi + h: dc * 32 + 16 * gi + h + 1],
                      src[h:h + 1, :].rearrange("a b -> b a"), "wgsem",
                      deps=(t_wg0,), allow_slow_non_contiguous=True)
        WGTOK = ("wgsem", p.cnt["wgsem"])

        try:
            with ExitStack() as es_att:
                def sba(name, shape, dt):
                    return es_att.enter_context(nc.sbuf_tensor(name, shape, dt))

                xT = sba("xT", [128, 8 * 2304], BF16)
                QT = sba("QT", [128, 8 * 2048], BF16)
                GQT = sba("GQT", [128, 4 * 2048], BF16)
                GKT = sba("GKT", [128, 8 * 64], BF16)
                GV2z = sba("GV2z", [128, 512], BF16)
                glgr = sba("glgr", [128, NT * 32], F32)

                es_b = ExitStack()
                def sbb(name, shape, dt):
                    return es_b.enter_context(nc.sbuf_tensor(name, shape, dt))
                xpT = sbb("xpT", [128, 8 * 64], BF16)
                x_in = [sbb(f"x_in{i}", [128, D], F32) for i in range(4)]
                wq_t = sbb("wq_t", [128, 8192], BF16)
                wgq_t = sbb("wgq_t", [128, 4096], BF16)
                wgk_t = sbb("wgk_t", [128, 4096], BF16)
                wgv_t = sbb("wgv_t", [128, 4096], BF16)
                GV2 = sbb("GV2", [64, 512], BF16)

                def wload(name, dst, src, deps=()):
                    p.new_sem("ws_" + name)
                    return p.dma("g", dst, src, "ws_" + name, deps=deps)

                WGK = wload("wgk", wgk_t[:, :], wgk_in)
                WGV = wload("wgv", wgv_t[:, :], wgv_in)
                WQ = wload("wq", wq_t[:, :], wq_in)
                WGQ = wload("wgq", wgq_t[:, :], wgq_in)

                for i in range(4):
                    p.new_sem(f"xsem{i}")
                x_slot_rd = [None] * 4
                tev = [None, None]
                for i in range(34):
                    sl = i % 4
                    deps = [x_slot_rd[sl]] if x_slot_rd[sl] else []
                    XL = p.dma("y", x_in[sl][:, :], x_comb[i, :, :], f"xsem{sl}", deps=deps)
                    if i < 18:
                        for j in range(2):
                            bk = j % 2
                            for dc in range(4 * j, 4 * j + 4):
                                p.op("t", lambda e, b=PB[bk], dc=dc, sl=sl:
                                     e.transpose(b[:, (dc % 4) * 128:(dc % 4) * 128 + 128],
                                                 x_in[sl][:, dc * 128:(dc + 1) * 128], id_f[:, :]),
                                     deps=[XL, CONST, tev[bk]])
                            tk = p.mark("t")
                            dst = (xT[:, :].rearrange("p (dc c) -> p dc c", c=2304)
                                   [:, 4 * j:4 * j + 4, i * 128:(i + 1) * 128])
                            src = PB[bk][:, :].rearrange("p (a b) -> p a b", b=128)
                            tev[bk] = p.op("v", lambda e, d=dst, s=src: e.tensor_copy(d, s),
                                           deps=[tk], mark=True)
                    if i >= 2:
                        j2 = i - 2
                        for dc in range(8):
                            p.op("t", lambda e, sl=sl, dc=dc, j2=j2:
                                 e.matmul(PB[7][:, dc * 64 + 2 * j2: dc * 64 + 2 * j2 + 2],
                                          x_in[sl][:, dc * 128:(dc + 1) * 128],
                                          poolmat[:, :], start=True, stop=True),
                                 deps=[XL, CONST])
                        x_slot_rd[sl] = p.mark("t")
                    else:
                        x_slot_rd[sl] = tk
                POOLT = x_slot_rd[33 % 4]
                XTD = [tev[0], tev[1]]
                XP = p.op("v", lambda e: e.tensor_copy(xpT[:, :], PB[7][:, :]),
                          deps=[POOLT], mark=True)
                dump("xpT", xpT[:, :], [128, 512], BF16, [XP])

                if KSTAGE < 1:
                    es_b.close()
                    es_att.close()
                    raise _SkipRest()
                for gc in range(4):
                    for kc in range(8):
                        p.op("t", lambda e, gc=gc, kc=kc:
                             e.matmul(PB[5][:, gc * 64:(gc + 1) * 64],
                                      wgk_t[:, kc * 512 + gc * 128: kc * 512 + gc * 128 + 128],
                                      xpT[:, kc * 64:(kc + 1) * 64],
                                      start=(kc == 0), stop=(kc == 7)),
                             deps=[XP, WGK])
                GKMM = p.mark("t")
                GKZ0 = p.op("v", lambda e: e.memset(GKT[:, :], 0.0), mark=True)
                for lh in range(8):
                    GKEV = p.op("v", lambda e, lh=lh:
                                e.tensor_copy(GKT[64 * (lh % 2):64 * (lh % 2) + 64,
                                                  lh * 64:(lh + 1) * 64],
                                              PB[5][64 * (lh % 2):64 * (lh % 2) + 64,
                                                    (lh // 2) * 64:(lh // 2) * 64 + 64]),
                                deps=[GKMM, GKZ0], mark=True)
                for kc in range(8):
                    p.op("t", lambda e, kc=kc:
                         e.matmul(PB[6][0:64, :], xpT[:, kc * 64:(kc + 1) * 64],
                                  wgv_t[:, kc * 512:(kc + 1) * 512],
                                  start=(kc == 0), stop=(kc == 7)),
                         deps=[XP, WGV])
                GVMM = p.mark("t")
                GVZ0 = p.op("v", lambda e: e.memset(GV2z[:, :], 0.0), mark=True)
                GVEV = p.op("v", lambda e: e.tensor_copy(GV2[0:64, :], PB[6][0:64, :]),
                            deps=[GVMM], mark=True)
                p.new_sem("gv2sem")
                ev_dst = GV2z[0:64, :].rearrange("p (l c) -> p l c", c=128)[:, :, 0:64]
                ev_src = GV2[0:64, :].rearrange("p (l c) -> p l c", c=128)[:, :, 0:64]
                od_dst = GV2z[64:128, :].rearrange("p (l c) -> p l c", c=128)[:, :, 64:128]
                od_src = GV2[0:64, :].rearrange("p (l c) -> p l c", c=128)[:, :, 64:128]
                p.dma("y", ev_dst, ev_src, "gv2sem", deps=[GVEV, GVZ0])
                GV2D = p.dma("y", od_dst, od_src, "gv2sem", deps=[GVEV, GVZ0])
                dump("GKT", GKT[:, :], [128, 512], BF16, [GKEV])
                dump("GV2", GV2z[:, :], [128, 512], BF16, [GV2D])

                qev = [None, None]
                for dc in range(8):
                    for n4 in range(4):
                        bk = (dc * 4 + n4) % 2
                        for kc in range(8):
                            p.op("t", lambda e, dc=dc, n4=n4, kc=kc, b=PB[2 + bk]:
                                 e.matmul(b[:, :],
                                          wq_t[:, kc * 1024 + dc * 128: kc * 1024 + dc * 128 + 128],
                                          xT[:, kc * 2304 + 256 + n4 * 512: kc * 2304 + 256 + (n4 + 1) * 512],
                                          start=(kc == 0), stop=(kc == 7)),
                                 deps=[XTD[0], XTD[1], WQ, qev[bk]])
                        tk = p.mark("t")
                        qev[bk] = p.op("v", lambda e, dc=dc, n4=n4, b=PB[2 + bk]:
                                       e.tensor_copy(QT[:, dc * 2048 + n4 * 512: dc * 2048 + (n4 + 1) * 512],
                                                     b[:, :]),
                                       deps=[tk], mark=True)
                QDONE = [qev[0], qev[1]]
                for dc in range(4):
                    for n4 in range(4):
                        bk = (dc * 4 + n4) % 2
                        for kc in range(8):
                            p.op("t", lambda e, dc=dc, n4=n4, kc=kc, b=PB[2 + bk]:
                                 e.matmul(b[:, :],
                                          wgq_t[:, kc * 512 + dc * 128: kc * 512 + dc * 128 + 128],
                                          xT[:, kc * 2304 + 256 + n4 * 512: kc * 2304 + 256 + (n4 + 1) * 512],
                                          start=(kc == 0), stop=(kc == 7)),
                                 deps=[XTD[0], XTD[1], WGQ, qev[bk]])
                        tk = p.mark("t")
                        qev[bk] = p.op("v", lambda e, dc=dc, n4=n4, b=PB[2 + bk]:
                                       e.tensor_copy(GQT[:, dc * 2048 + n4 * 512: dc * 2048 + (n4 + 1) * 512],
                                                     b[:, :]),
                                       deps=[tk], mark=True)
                GQDONE = [qev[0], qev[1]]
                dump("QT", QT[:, 0:2048], [128, 2048], BF16, QDONE)
                for tc in range(NT):
                    for kc in range(8):
                        p.op("t", lambda e, tc=tc, kc=kc:
                             e.matmul(PB[4][:, tc * 32:(tc + 1) * 32],
                                      QT[:, kc * 2048 + tc * 128: kc * 2048 + tc * 128 + 128],
                                      Wg[:, kc * 32:(kc + 1) * 32],
                                      start=(kc == 0), stop=(kc == 7)),
                             deps=[QDONE[0], QDONE[1], WGTOK])
                GMM = p.mark("t")
                GSIG = p.op("s", lambda e: e.activation(glgr[:, :], PB[4][:, :], AF.Sigmoid),
                            deps=[GMM], mark=True)
                dump("glgr", glgr[:, :], [128, 512], F32, [GSIG])

                if KSTAGE < 2:
                    es_b.close()
                    es_att.close()
                    raise _SkipRest()
                es_b.close()
                es_e = ExitStack()
                def sbe(name, shape, dt):
                    return es_e.enter_context(nc.sbuf_tensor(name, shape, dt))
                wk_t = sbe("wk_t", [128, 8192], BF16)
                wv_t = sbe("wv_t", [128, 8192], BF16)
                wo_t = sbe("wo_t", [128, 8192], BF16)
                ro8 = sbe("ro8", [64, 8192], BF16)
                Kroll = sbe("Kroll", [128, 16 * 512], BF16)
                Vroll = [sbe(f"Vroll{i}", [128, 16 * 65], BF16) for i in range(4)]
                Pt = [sbe(f"Pt{i}", [128, 512], BF16) for i in range(3)]
                lp_tok = sbe("lp_tok", [128, 512], BF16)
                latT8 = sbe("latT8", [64, 1024], BF16)
                comb = sbe("comb", [128, D], BF16)
                x1_tc = sbe("x1_tc", [128, D], F32)
                x1T_tc = sbe("x1T_tc", [128, D], BF16)
                x1Tf = sbe("x1Tf", [128, D], F32)
                maskT = sbe("maskT_s", [128, 4 * 512], BF16)
                latmask = sbe("latmask_s", [128, NT * 64], BF16)
                lpn = lp_tok
                lptn = lp_tok

                TCD = tk
                DBGT = max((dbg[k] for k in dbg), key=lambda t: 0) if False else None
                if DBG:
                    DBGT = None
                    dbg_gates = [dbg[k] for k in dbg]
                else:
                    dbg_gates = []
                GVDMA = ("gv2sem", p.cnt["gv2sem"])
                WK = wload("wk", wk_t[:, :], wk_in, deps=[TCD, GVDMA] + dbg_gates)
                WV = wload("wv", wv_t[:, :], wv_in, deps=[TCD, GVDMA] + dbg_gates)
                WO = wload("wo", wo_t[:, :], wo_in, deps=[TCD, GVDMA] + dbg_gates)
                RO = wload("ro", ro8[:, :], ro_in, deps=[TCD, GVDMA] + dbg_gates)
                p.new_sem("msksem")
                MASKS = p.dma("y", maskT[:, :], maskT_in, "msksem", deps=[TCD, GVDMA] + dbg_gates)
                MASKS = p.dma("y", latmask[:, :], latmask_in, "msksem", deps=[TCD, GVDMA] + dbg_gates)
                MASKS = ("msksem", p.cnt["msksem"])
                VR1 = [p.op("v", lambda e, t=t: e.memset(t[:, :], 1.0),
                            deps=[TCD, GVDMA] + dbg_gates, mark=True)
                       for t in Vroll]
                KR0 = p.op("v", lambda e: e.memset(Kroll[:, :], 0.0),
                           deps=[TCD, GVDMA] + dbg_gates, mark=True)

                kroll_rd = [[KR0], [KR0]]
                kroll_wr = [KR0, KR0]
                vroll_rd = [VR1[0], VR1[1], VR1[2], VR1[3]]
                vroll_wr = [VR1[0], VR1[1], VR1[2], VR1[3]]
                pt_rd = [None] * 3
                comb_rd = None
                dbg_comb = None
                x1war = []
                x1T_rd = None
                x1Tf_rd = None
                sq_prev = None
                p.new_sem("xresem")
                p.new_sem("x1sem")
                p.new_sem("x1tsem0")
                p.new_sem("x1tsem1")

                st_ev = [GKEV, None, None]
                STB = [0, 1, 5]
                loc_ev = [GSIG] * 4
                lsc_ev = GSIG
                lat_ev = [GVEV, XP]
                lpt_ev = GKEV
                pb5_ex = GKEV
                rem_rd = [GQDONE[0], GQDONE[1]]
                wo_rd = [GQDONE[0], GQDONE[1]]
                pb23_rd = [GQDONE[0], GQDONE[1]]
                combT_ev = GVEV
                rtr_rd = [GKEV, GKEV]
                kv_ev = [None, None, GQDONE[0], GQDONE[1]]
                CC = None
                LEXP = None
                pt_dump = None
                dbg_kroll = None
                LATMM_prev = None
                REMMM_prev = None
                COMBDONE_prev = None

                S = {"lsc_ev": GSIG, "lexp": {}, "latmm": None}

                def emit_roll(tc2):
                    if tc2 % 2 != 0:
                        return
                    for b in ([-1, 0] if tc2 == 0 else [tc2 // 2]):
                        slot = b % 2
                        for mc in range(8):
                            bk = mc % 2
                            for kc in range(8):
                                p.op("t", lambda e, mc=mc, kc=kc, b=b, bkk=PB[bk]:
                                     e.matmul(bkk[:, 0:256],
                                              wk_t[:, kc * 1024 + mc * 128: kc * 1024 + mc * 128 + 128],
                                              xT[:, kc * 2304 + 256 + b * 256: kc * 2304 + 256 + b * 256 + 256],
                                              start=(kc == 0), stop=(kc == 7)),
                                     deps=[XTD[0], XTD[1], WK, st_ev[bk], kv_ev[bk]])
                            tk = p.mark("t")
                            p.op("v", lambda e, mc=mc, slot=slot, bkk=PB[bk]:
                                 e.tensor_copy(Kroll[0:64, (2 * mc) * 512 + slot * 256:
                                                     (2 * mc) * 512 + slot * 256 + 256],
                                               bkk[0:64, 0:256]),
                                 deps=[tk] + kroll_rd[slot])
                            kv_ev[bk] = p.op(
                                "v", lambda e, mc=mc, slot=slot, bkk=PB[bk]:
                                e.tensor_copy(Kroll[64:128, (2 * mc + 1) * 512 + slot * 256:
                                                    (2 * mc + 1) * 512 + slot * 256 + 256],
                                              bkk[64:128, 0:256]),
                                deps=[tk], mark=True)
                        kroll_wr[slot] = max(kv_ev[0], kv_ev[1], key=lambda t: t[1])
                        for tt in range(2):
                            vt = slot * 2 + tt
                            for nh in range(2):
                                bk = 2 + (tt * 2 + nh) % 2
                                for kc in range(8):
                                    p.op("t", lambda e, tt=tt, nh=nh, kc=kc, b=b, bkk=PB[bk]:
                                         e.matmul(bkk[:, :],
                                                  xT[:, kc * 2304 + 256 + b * 256 + tt * 128:
                                                     kc * 2304 + 256 + b * 256 + tt * 128 + 128],
                                                  wv_t[:, kc * 1024 + nh * 512: kc * 1024 + (nh + 1) * 512],
                                                  start=(kc == 0), stop=(kc == 7)),
                                         deps=[WV, rem_rd[bk - 2], pb23_rd[bk - 2], kv_ev[bk]])
                                tk = p.mark("t")
                                dst = (Vroll[vt][:, :].rearrange("p (h c) -> p h c", c=65)
                                       [:, nh * 8:(nh + 1) * 8, 0:64])
                                src = PB[bk][:, :].rearrange("p (h c) -> p h c", c=64)
                                kv_ev[bk] = p.op("v", lambda e, d=dst, s=src: e.tensor_copy(d, s),
                                                 deps=[tk, vroll_rd[vt]], mark=True)
                            vroll_wr[vt] = max(kv_ev[2], kv_ev[3], key=lambda t: t[1])

                def emit_lsc(tc2):
                    for lh in range(NLH):
                        p.op("t", lambda e, lh=lh, tc=tc2:
                             e.matmul(PB[4][:, lh * 64:(lh + 1) * 64],
                                      GQT[:, (lh // 2) * 2048 + tc * 128:
                                          (lh // 2) * 2048 + tc * 128 + 128],
                                      GKT[:, lh * 64:(lh + 1) * 64],
                                      start=True, stop=True),
                             deps=[GQDONE[0], GQDONE[1], GKEV, S["lsc_ev"]] + loc_ev)
                    LSCMM = p.mark("t")
                    lx = p.op("s", lambda e:
                              e.activation(lp_tok[:, :], PB[4][:, :], AF.Exp, scale=0.125),
                              deps=[LSCMM, S["lexp"].get(tc2 - 1), S["latmm"]], mark=True)
                    S["lexp"][tc2] = lx
                    S["lsc_ev"] = lx

                emit_roll(0)
                emit_lsc(0)
                dbg_kroll = dump("Kroll0", Kroll[:, :], [128, 8192], BF16,
                                 [kroll_wr[0], kroll_wr[1]])

                for tc in range(min(NT, KTC)):
                    n, par = tc // 2, tc % 2
                    cslot = tc % 2
                    LEXP = S["lexp"][tc]
                    lsc_ev = LEXP
                    lm = []
                    for lh in range(NLH):
                        lm.append(p.op("v", lambda e, lh=lh, tc=tc:
                                       e.tensor_tensor(lp_tok[:, lh * 64:(lh + 1) * 64],
                                                       lp_tok[:, lh * 64:(lh + 1) * 64],
                                                       latmask[:, tc * 64:(tc + 1) * 64], ALU.mult),
                                       deps=[LEXP, MASKS], mark=True))
                    sums = []
                    for lh in range(NLH):
                        sums.append(p.op("v", lambda e, lh=lh:
                                         e.reduce_sum(stat[:, 8 + lh:9 + lh],
                                                      lp_tok[:, lh * 64:(lh + 1) * 64], axis=AX),
                                         deps=[lm[lh]], mark=True))
                    if KLAT < 1:
                        continue
                    RCP = p.op("v", lambda e: e.reciprocal(stat[:, 16:24], stat[:, 8:16]),
                               deps=sums, mark=True)
                    nrm = []
                    for lh in range(NLH):
                        nrm.append(p.op("v", lambda e, lh=lh:
                                        e.tensor_scalar(x1Tf[:, lh * 64:(lh + 1) * 64],
                                                        lp_tok[:, lh * 64:(lh + 1) * 64],
                                                        stat[:, 16 + lh:17 + lh], None, ALU.mult),
                                        deps=[RCP, x1Tf_rd, sq_prev], mark=True))
                    if KLAT < 2:
                        continue
                    lpdump = dump("lp_tok0", x1Tf[:, 0:512], [128, 512], F32, nrm) \
                        if tc == 0 else None
                    for j in range(4):
                        p.op("t", lambda e, j=j:
                             e.transpose(PB[5][:, j * 128:(j + 1) * 128],
                                         x1Tf[:, j * 128:(j + 1) * 128], id_f[:, :]),
                             deps=[nrm[2 * j], nrm[2 * j + 1], CONST, lpt_ev, pb5_ex,
                                   rtr_rd[0], rtr_rd[1], lpdump])
                    TTS = p.mark("t")
                    LPTEV = p.op("v", lambda e: e.tensor_copy(lptn[:, :], PB[5][:, :]),
                                 deps=[TTS] + nrm, mark=True)
                    lpt_ev = LPTEV
                    if KLAT < 3:
                        continue
                    for lh in range(NLH):
                        p.op("t", lambda e, lh=lh:
                             e.matmul(PB[6 + lh // 4][0:64, (lh % 4) * 128:(lh % 4) * 128 + 128],
                                      GV2z[:, 64 * lh: 64 * lh + 64],
                                      lptn[:, (lh // 2) * 128:(lh // 2) * 128 + 128],
                                      start=True, stop=True),
                             deps=[LPTEV, GV2D, lat_ev[lh // 4], combT_ev, COMBDONE_prev])
                    LATMM = p.mark("t")
                    LATMM_prev = LATMM
                    S["latmm"] = LATMM
                    LATEV0 = p.op("v", lambda e: e.tensor_copy(latT8[0:64, 0:512], PB[6][0:64, 0:512]),
                                  deps=[LATMM, REMMM_prev], mark=True)
                    LATEV1 = p.op("v", lambda e: e.tensor_copy(latT8[0:64, 512:1024], PB[7][0:64, 0:512]),
                                  deps=[LATMM], mark=True)
                    lat_ev = [LATEV0, LATEV1]
                    if KLAT < 4:
                        continue
                    for lh in range(NLH):
                        for nh2 in range(2):
                            p.op("t", lambda e, lh=lh, nh2=nh2:
                                 e.matmul(PB[2 + nh2][:, :],
                                          latT8[0:64, lh * 128:(lh + 1) * 128],
                                          ro8[0:64, lh * 1024 + nh2 * 512: lh * 1024 + (nh2 + 1) * 512],
                                          start=(lh == 0), stop=(lh == 7)),
                                 deps=[lat_ev[lh // 4], RO, rem_rd[nh2], pb23_rd[nh2], kv_ev[2 + nh2]])
                    REMMM = p.mark("t")
                    REMMM_prev = REMMM
                    if KLAT < 5:
                        continue
                    rg = []
                    for h in range(NH):
                        rg.append(p.op("v", lambda e, h=h, tc=tc:
                                       e.tensor_scalar(comb[:, h * 64:(h + 1) * 64],
                                                       PB[2 + h // 8][:, (h % 8) * 64:(h % 8) * 64 + 64],
                                                       glgr[:, tc * 32 + 16 + h: tc * 32 + 17 + h],
                                                       None, ALU.mult),
                                       deps=[REMMM, GSIG, comb_rd, dbg_comb],
                                       mark=True))
                    rem_rd[:] = [rg[7], rg[15]]
                    if tc == 0:
                        dtk = dump("comb_rem0", comb[:, :], [128, 1024], BF16, rg)
                        if dtk:
                            rg = [max(t, dtk, key=lambda z: (z[0] != "pv", )) if False else t
                                  for t in rg]
                            dbg_comb = dtk
                            rg = [dtk if False else t for t in rg]
                            rg_gate = dtk
                        else:
                            rg_gate = None
                    else:
                        rg_gate = None

                    if tc == 0:
                        dtk = dump("latT80", latT8[0:64, :], [64, 1024], BF16, lat_ev)
                        if dtk:
                            REMMM_prev = dtk

                    if KPART < 2:
                        continue
                    fin_toks = []
                    sm = {}

                    ACT = (0, 1, 2) if par == 0 else (1, 2, 3)
                    cbase = 0 if par == 0 else 128

                    def emit_st(h):
                        i3 = h % 3
                        bkid = STB[i3]
                        extra = []
                        if bkid in (0, 1):
                            extra = [kv_ev[bkid]]
                        else:
                            extra = [lpt_ev, rtr_rd[0], rtr_rd[1]]
                        for kc in ACT:
                            bb = n - 1 + kc // 2
                            kslot = bb % 2
                            p.op("t", lambda e, h=h, kc=kc, kslot=kslot, tc=tc, bkk=PB[bkid]:
                                 e.matmul(bkk[:, kc * 128:(kc + 1) * 128],
                                          Kroll[:, h * 512 + kslot * 256 + (kc % 2) * 128:
                                                h * 512 + kslot * 256 + (kc % 2) * 128 + 128],
                                          QT[:, (h // 2) * 2048 + tc * 128:
                                             (h // 2) * 2048 + tc * 128 + 128],
                                          start=True, stop=True),
                                 deps=[kroll_wr[kslot], QDONE[0], QDONE[1], st_ev[i3]] + extra)
                        return p.mark("t")

                    def emit_sm(h, stmm):
                        nonlocal pb5_ex
                        i3 = h % 3
                        EX = p.op("s", lambda e, i3=i3, cb=cbase:
                                  e.activation(Pt[i3][:, cb:cb + 384],
                                               PB[STB[i3]][:, cb:cb + 384], AF.Exp, scale=0.125),
                                  deps=[stmm, pt_rd[i3], pt_dump if i3 == 0 else None], mark=True)
                        if STB[i3] == 5:
                            pb5_ex = EX
                        var = par + 2 * (1 if n == 0 else 0)
                        MK = p.op("v", lambda e, i3=i3, var=var, cb=cbase:
                                  e.tensor_tensor(Pt[i3][:, cb:cb + 384],
                                                  Pt[i3][:, cb:cb + 384],
                                                  maskT[:, var * 512 + cb: var * 512 + cb + 384],
                                                  ALU.mult),
                                  deps=[EX, MASKS], mark=True)
                        return EX, MK

                    def emit_pv(h):
                        nonlocal pt_dump
                        if tc == 0 and h == 0 and pt_dump is None:
                            pt_dump = dump("Pt00", Pt[0][:, 0:384], [128, 384], BF16, [sm[0][1]])
                            if pt_dump:
                                pt_rd[0] = pt_dump
                        i3 = h % 3
                        i4 = h % 4
                        lb = 4 if h % 2 == 0 else 6
                        col = 128 * ((h // 2) % 2)
                        mk = sm[h][1]
                        for kc in ACT:
                            bb = n - 1 + kc // 2
                            vt = (bb % 2) * 2 + (kc % 2)
                            p.op("t", lambda e, h=h, kc=kc, vt=vt, i3=i3, lb=lb, col=col,
                                 s_=(kc == ACT[0]), t_=(kc == ACT[-1]):
                                 e.matmul(PB[lb][:, col: col + 65],
                                          Pt[i3][:, kc * 128:(kc + 1) * 128],
                                          Vroll[vt][:, h * 65: h * 65 + 65],
                                          start=s_, stop=t_),
                                 deps=[mk, vroll_wr[vt], loc_ev[h % 2], lsc_ev,
                                       lat_ev[0] if lb == 6 else None])
                        tk = p.mark("t")
                        st_ev[i3] = tk
                        pt_rd[i3] = tk
                        return tk

                    def emit_fin(h, pvtk):
                        i4 = h % 4
                        lb = 4 if h % 2 == 0 else 6
                        col = 128 * ((h // 2) % 2)
                        R = p.op("v", lambda e, lb=lb, col=col:
                                 e.reciprocal(stat[:, 0:1], PB[lb][:, col + 64: col + 65]),
                                 deps=[pvtk], mark=True)
                        G = p.op("v", lambda e, h=h, tc=tc:
                                 e.tensor_tensor(stat[:, 1:2], glgr[:, tc * 32 + h: tc * 32 + h + 1],
                                                 stat[:, 0:1], ALU.mult),
                                 deps=[R, GSIG], mark=True)
                        A = p.op("v", lambda e, h=h, lb=lb, col=col:
                                 e.scalar_tensor_tensor(comb[:, h * 64:(h + 1) * 64],
                                                        PB[lb][:, col: col + 64],
                                                        stat[:, 1:2],
                                                        comb[:, h * 64:(h + 1) * 64],
                                                        ALU.mult, ALU.add),
                                 deps=[G, rg[h], rg_gate], mark=True)
                        loc_ev[h % 2] = A
                        fin_toks.append(A)

                    PIPE = int(os.environ.get('KPIPE', '2'))
                    last_pv = None
                    for h in range(NH + PIPE):
                        if h < NH:
                            stm = emit_st(h)
                            sm[h] = (None, None)
                            sm[h] = emit_sm(h, stm)
                        if h >= PIPE:
                            hh = h - PIPE
                            last_pv = emit_pv(hh)
                            emit_fin(hh, last_pv)
                    COMBDONE = fin_toks[-1]
                    COMBDONE_prev = COMBDONE
                    kroll_rd[:] = [[last_pv, dbg_kroll], [last_pv, dbg_kroll]]
                    vroll_rd[:] = [last_pv] * 4

                    if KPART < 3:
                        continue
                    CB0 = p.op("v", lambda e: e.tensor_copy(x1Tf[:, 0:512], comb[:, 0:512]),
                               deps=[COMBDONE, TTS, x1Tf_rd, sq_prev], mark=True)
                    CB1 = p.op("v", lambda e: e.tensor_copy(x1Tf[:, 512:1024], comb[:, 512:1024]),
                               deps=[COMBDONE], mark=True)
                    ctev = []
                    for rnd in range(2):
                        for q4 in range(4):
                            dc = rnd * 4 + q4
                            p.op("t", lambda e, dc=dc, q4=q4:
                                 e.transpose(PB[6][:, q4 * 128:(q4 + 1) * 128],
                                             x1Tf[:, dc * 128:(dc + 1) * 128], id_f[:, :]),
                                 deps=[CB0, CB1, CONST, lat_ev[0], combT_ev])
                        tk = p.mark("t")
                        cte = p.op("v", lambda e, rnd=rnd:
                                   e.tensor_copy(comb[:, rnd * 512:(rnd + 1) * 512],
                                                 PB[6][:, :]),
                                   deps=[tk, dbg_comb], mark=True)
                        ctev.append(cte)
                        combT_ev = cte
                    COMBT_T = tk
                    for nh2 in range(2):
                        for kc in range(8):
                            p.op("t", lambda e, nh2=nh2, kc=kc:
                                 e.matmul(PB[6 + nh2][:, :],
                                          comb[:, kc * 128:(kc + 1) * 128],
                                          wo_t[:, kc * 1024 + nh2 * 512: kc * 1024 + (nh2 + 1) * 512],
                                          start=(kc == 0), stop=(kc == 7)),
                                 deps=[ctev[0], ctev[1], WO, lat_ev[1]])
                    WOMM = p.mark("t")
                    comb_rd = WOMM

                    if KPART < 4:
                        continue
                    XRE = p.dma("y", x1_tc[:, :], x_comb[2 + tc, :, :], "xresem",
                                deps=x1war)
                    ADD0 = p.op("v", lambda e:
                                e.scalar_tensor_tensor(x1_tc[:, 0:512], x1_tc[:, 0:512],
                                                       1.0, PB[6][:, :], ALU.mult, ALU.add,
                                                       accum_out=stat[:, 32:33]),
                                deps=[WOMM, XRE], mark=True)
                    ADD1 = p.op("v", lambda e:
                                e.scalar_tensor_tensor(x1_tc[:, 512:1024], x1_tc[:, 512:1024],
                                                       1.0, PB[7][:, :], ALU.mult, ALU.add,
                                                       accum_out=stat[:, 33:34]),
                                deps=[WOMM, XRE], mark=True)
                    wo_rd[:] = [ADD0, ADD1]
                    combT_ev = ADD1
                    if tc + 1 < min(NT, KTC):
                        emit_roll(tc + 1)
                        emit_lsc(tc + 1)
                    SQ = p.op("s", lambda e:
                              e.activation(x1Tf[:, :], x1_tc[:, :], AF.Square,
                                           accum_out=stat[:, 4:5]),
                              deps=[ADD0, ADD1, sq_prev, x1Tf_rd, COMBT_T], mark=True)
                    sq_prev = SQ
                    NM = p.op("v", lambda e: e.scalar_tensor_tensor(
                                  stat[:, 3:4], stat[:, 32:33], -1.0 / 1024,
                                  stat[:, 33:34].to_broadcast((128, 1)) if False else stat[:, 33:34],
                                  ALU.bypass, ALU.bypass),
                              deps=[ADD0, ADD1], mark=True) if False else None
                    NM1 = p.op("v", lambda e: e.tensor_tensor(stat[:, 34:35], stat[:, 32:33],
                                                              stat[:, 33:34], ALU.add),
                               deps=[ADD0, ADD1], mark=True)
                    NM2 = p.op("v", lambda e: e.tensor_scalar(stat[:, 3:4], stat[:, 34:35],
                                                              -1.0 / 1024, None, ALU.mult),
                               deps=[NM1], mark=True)
                    M2C = p.op("v", lambda e: e.tensor_tensor(stat[:, 35:36], stat[:, 3:4],
                                                              stat[:, 3:4], ALU.mult),
                               deps=[NM2], mark=True)
                    VARC = p.op("v", lambda e: e.tensor_scalar(stat[:, 36:37], stat[:, 4:5],
                                                               1.0 / 1024, None, ALU.mult),
                               deps=[SQ], mark=True)
                    VARF = p.op("v", lambda e: e.tensor_tensor(stat[:, 37:38], stat[:, 36:37],
                                                               stat[:, 35:36], ALU.subtract),
                               deps=[VARC, M2C], mark=True)
                    LNV = p.op("s", lambda e: e.activation(stat[:, 5:6], stat[:, 37:38], AF.Ln,
                                                           bias=eps_c[:, :]),
                               deps=[VARF, EPSC], mark=True)
                    RSD = p.op("s", lambda e: e.activation(stat[:, 6:7], stat[:, 5:6], AF.Exp,
                                                           scale=-0.5),
                               deps=[LNV], mark=True)
                    NMR = p.op("v", lambda e: e.tensor_tensor(stat[:, 7:8], stat[:, 3:4],
                                                              stat[:, 6:7], ALU.mult),
                               deps=[RSD, NM2], mark=True)
                    X1 = p.op("v", lambda e:
                              e.tensor_scalar(x1_tc[:, :], x1_tc[:, :], stat[:, 6:7],
                                              stat[:, 7:8], ALU.mult, ALU.add),
                              deps=[NMR, SQ], mark=True)

                    X1T = []
                    X1TF = []
                    x1t_t = None
                    for rnd in range(2):
                        tb = 2 + rnd
                        for q4 in range(4):
                            dc = rnd * 4 + q4
                            p.op("t", lambda e, dc=dc, q4=q4, tb=tb:
                                 e.transpose(PB[tb][:, q4 * 128:(q4 + 1) * 128],
                                             x1_tc[:, dc * 128:(dc + 1) * 128], id_f[:, :]),
                                 deps=[X1, kv_ev[2], kv_ev[3], rem_rd[0], rem_rd[1]])
                        x1t_t = p.mark("t")
                        EB = p.op("v", lambda e, rnd=rnd, tb=tb:
                                  e.tensor_copy(x1T_tc[:, rnd * 512:(rnd + 1) * 512], PB[tb][:, :]),
                                  deps=[x1t_t, x1T_rd], mark=True)
                        EF = p.op("v", lambda e, rnd=rnd, tb=tb:
                                  e.tensor_copy(x1Tf[:, rnd * 512:(rnd + 1) * 512], PB[tb][:, :]),
                                  deps=[x1t_t, x1Tf_rd, sq_prev], mark=True)
                        X1T.append(EB)
                        X1TF.append(EF)
                    pb23_rd[:] = [X1TF[0], X1TF[1]]

                    if KPART < 5:
                        continue
                    rslot = tc % 2
                    roff = 256 + rslot * 16
                    for kc in range(8):
                        p.op("t", lambda e, kc=kc, roff=roff:
                             e.matmul(PB[5][:, roff:roff + 12],
                                      x1Tf[:, kc * 128:(kc + 1) * 128],
                                      lrw[:, kc * 12:(kc + 1) * 12],
                                      start=(kc == 0), stop=False),
                             deps=X1TF + [CONST, rtr_rd[rslot], pb5_ex])
                    p.op("t", lambda e, roff=roff:
                         e.matmul(PB[5][:, roff:roff + 12], ones_f[:, :], lrb[:, :],
                                  start=False, stop=True),
                         deps=[ONESF, CONST])
                    RMM = p.mark("t")
                    x1Tf_rd = RMM
                    E4 = p.op("s", lambda e, roff=roff:
                              e.activation(lp_pad[:, 0:4], PB[5][:, roff:roff + 4], AF.Exp,
                                           accum_out=stat[:, 24:25]),
                              deps=[RMM, LPP0], mark=True)
                    R4 = p.op("v", lambda e: e.reciprocal(stat[:, 25:26], stat[:, 24:25]),
                              deps=[E4], mark=True)
                    LP4 = p.op("v", lambda e: e.tensor_scalar(lp_pad[:, 0:4], lp_pad[:, 0:4],
                                                              stat[:, 25:26], None, ALU.mult),
                               deps=[R4], mark=True)
                    EW0 = p.op("v", lambda e: e.tensor_tensor(rl_sb[:, 0:4], lp_pad[:, 0:4],
                                                              lp_pad[:, 1:5], ALU.add),
                               deps=[LP4, LPP0], mark=True)
                    EW1 = p.op("v", lambda e: e.tensor_tensor(rl_sb[:, 4:8], lp_pad[:, 0:4],
                                                              lp_pad[:, 1:5], ALU.add),
                               deps=[LP4, LPP0], mark=True)
                    RL = p.op("v", lambda e, roff=roff:
                              e.scalar_tensor_tensor(rl_sb[:, :], rl_sb[:, :], 0.1,
                                                     PB[5][:, roff + 4:roff + 12], ALU.mult, ALU.add),
                              deps=[EW0, EW1, RMM], mark=True)
                    E8 = p.op("s", lambda e: e.activation(p8_sb[:, :], rl_sb[:, :], AF.Exp,
                                                          accum_out=stat[:, 26:27]),
                              deps=[RL], mark=True)
                    R8 = p.op("v", lambda e: e.reciprocal(stat[:, 27:28], stat[:, 26:27]),
                              deps=[E8], mark=True)
                    PP = p.op("v", lambda e: e.tensor_scalar(p8_sb[:, :], p8_sb[:, :],
                                                             stat[:, 27:28], None, ALU.mult),
                              deps=[R8], mark=True)
                    M1 = p.op("v", lambda e: e.reduce_max(stat[:, 28:29], p8_sb[:, :], axis=AX),
                              deps=[PP], mark=True)
                    MRa = p.op("v", lambda e: e.tensor_scalar(pm_sb[:, :], p8_sb[:, :],
                                                              stat[:, 28:29], None, ALU.is_ge),
                               deps=[M1, PP], mark=True)
                    MR = p.op("v", lambda e: e.scalar_tensor_tensor(pm_sb[:, :], pm_sb[:, :],
                                                                    -2.0, p8_sb[:, :],
                                                                    ALU.mult, ALU.add),
                              deps=[MRa], mark=True)
                    M2 = p.op("v", lambda e: e.reduce_max(stat[:, 29:30], pm_sb[:, :], axis=AX),
                              deps=[MR], mark=True)
                    S12 = p.op("v", lambda e: e.tensor_tensor(stat[:, 30:31], stat[:, 28:29],
                                                              stat[:, 29:30], ALU.add),
                               deps=[M2, M1], mark=True)
                    R12 = p.op("v", lambda e: e.reciprocal(stat[:, 31:32], stat[:, 30:31]),
                               deps=[S12], mark=True)
                    GEm = p.op("v", lambda e: e.tensor_scalar(ge_sb[:, :], p8_sb[:, :],
                                                              stat[:, 29:30], None, ALU.is_ge),
                               deps=[M2, PP], mark=True)
                    CC = p.op("v", lambda e, tc=tc:
                              e.scalar_tensor_tensor(c_sb[:, tc * 8:(tc + 1) * 8], p8_sb[:, :],
                                                     stat[:, 31:32], ge_sb[:, :], ALU.mult, ALU.mult),
                              deps=[R12, GEm], mark=True)
                    p.op("t", lambda e, tc=tc:
                         e.transpose(PB[5][0:8, 288:416], c_sb[:, tc * 8:(tc + 1) * 8], id_f[:, :]),
                         deps=[CC, rtr_rd[1 - rslot]])
                    CTMM = p.mark("t")
                    CTEV = p.op("v", lambda e, tc=tc:
                                e.tensor_copy(cT[0:8, tc * 128:(tc + 1) * 128], PB[5][0:8, 288:416]),
                                deps=[CTMM], mark=True)
                    rtr_rd[rslot] = CTEV

                    SX1 = p.dma("y", x1s[tc, :, :], x1_tc[:, :], "x1sem", deps=[X1])
                    SXT = p.dma("y", x1Ts[:, :, tc * 128:(tc + 1) * 128].rearrange("k p m -> p k m"),
                                x1T_tc[:, :].rearrange("p (k m) -> p k m", k=8),
                                f"x1tsem{tc % 2}", deps=X1T)
                    x1war = [SX1, x1t_t, sq_prev, X1]
                    x1T_rd = SXT

                    if tc == 0:
                        dtk = dump("comb0", comb[:, :], [128, 1024], BF16, [COMBDONE] + rg)
                        if dtk:
                            dbg_comb = dtk
                        dtk = dump("x1_0", x1_tc[:, :], [128, 1024], F32, [X1])
                        if dtk:
                            x1war = x1war + [dtk]
                        dump("c0", c_sb[:, 0:8], [128, 8], F32, [CC])

                X1SALL = ("x1sem", p.cnt["x1sem"])
                X1TALL = ("x1tsem0", p.cnt["x1tsem0"])
                X1TALL1 = ("x1tsem1", p.cnt["x1tsem1"])
                es_e.close()
                FT = ("pt", p.cnt["pt"])
                FV = ("pv", p.cnt["pv"])
                FS = ("ps", p.cnt["ps"])
                FGATE = [FT, FV, FS, X1SALL, X1TALL, X1TALL1]
                if DBG:
                    FGATE.extend(dbg[k] for k in dbg)
                EPB = {
                    0: st_ev[0], 1: st_ev[1], 2: pb23_rd[0], 3: pb23_rd[1],
                    4: ("pv", p.cnt["pv"]), 5: rtr_rd[1],
                    6: wo_rd[0], 7: wo_rd[1],
                }
                CCLAST = CC
                CT_LAST = rtr_rd[1]
                EPB5X = pb5_ex

            if KSTAGE < 3:
                raise _SkipRest()
            with ExitStack() as es_ffn:
                def sbf(name, shape, dt):
                    return es_ffn.enter_context(nc.sbuf_tensor(name, shape, dt))

                acc = sbf("acc", [128, NT * 1024], F32)
                x1T_sb = sbf("x1T_sb", [128, 8 * 2048], BF16)
                hT = sbf("hT", [128, 32 * 1024], BF16)
                es_f1 = ExitStack()
                def sbf1(name, shape, dt):
                    return es_f1.enter_context(nc.sbuf_tensor(name, shape, dt))
                w1b = [sbf1(f"w1b{i}", [128, 1024], BF16) for i in range(3)]
                w2b = [sbf1(f"w2b{i}", [128, 8192], BF16) for i in range(2)]
                b1all = sbf1("b1all", [128, 256], F32)

                p.new_sem("xtsem")
                XTL = p.dma("y", x1T_sb[:, :].rearrange("p (k m) -> p k m", k=8),
                            x1Ts.rearrange("k p m -> p k m"), "xtsem", deps=FGATE)
                p.new_sem("accsem")
                ACCL = p.dma("y", acc[:, :].rearrange("p (t c) -> p t c", t=NT),
                             x1s.rearrange("t p c -> p t c"), "accsem", deps=FGATE)
                p.new_sem("b1sem")
                B1L = p.dma("y", b1all[:, :].rearrange("p (e f) -> p e f", e=8),
                            b1_in.rearrange("e p f -> p e f"), "b1sem", deps=FGATE)
                for i in range(3):
                    p.new_sem(f"w1sem{i}")
                for i in range(2):
                    p.new_sem(f"w2sem{i}")

                w1_war = [None, None, None]
                w2_war = [None, None]
                w1_tok = {}
                w2_tok = {}

                seq_w1 = [(e_, Tt, f) for e_ in range(NE) for Tt in range(2) for f in range(32)]
                seq_w2 = [(e_, Tt, dq) for e_ in range(NE) for Tt in range(2) for dq in range(4)]

                def load_w1(idx):
                    e_, Tt, f = seq_w1[idx]
                    sl = idx % 3
                    tok = p.dma("g", w1b[sl][:, :], w1_in[e_, f, :, :], f"w1sem{sl}",
                                deps=[w1_war[sl]] + (FGATE if idx < 3 else []))
                    w1_tok[idx] = (tok, sl)

                def load_w2(idx):
                    e_, Tt, dq = seq_w2[idx]
                    sl = idx % 2
                    tok = p.dma("g", w2b[sl][:, :], w2_in[e_, dq, :, :], f"w2sem{sl}",
                                deps=[w2_war[sl]] + (FGATE if idx < 2 else []))
                    w2_tok[idx] = (tok, sl)

                load_w1(0)
                load_w1(1)
                load_w1(2)
                load_w2(0)
                load_w2(1)

                o_ev = [EPB[4], EPB[5]]
                h_rd = [EPB[0], EPB[1], EPB[2], EPB[3]]
                MM2LAST = None
                iw1 = 0
                iw2 = 0

                for e_ in range(NE):
                    for Tt in range(2):
                        HDONE = None
                        for f in range(32):
                            tokw, sl = w1_tok[iw1]
                            tkmm = None
                            for nh in range(2):
                                bk = 2 * (f % 2) + nh
                                for kc in range(8):
                                    p.op("t", lambda e, f=f, nh=nh, kc=kc, sl=sl, Tt=Tt, bk=bk:
                                         e.matmul(PB[bk][:, :],
                                                  w1b[sl][:, kc * 128:(kc + 1) * 128],
                                                  x1T_sb[:, kc * 2048 + Tt * 1024 + nh * 512:
                                                         kc * 2048 + Tt * 1024 + (nh + 1) * 512],
                                                  start=(kc == 0), stop=(kc == 7)),
                                         deps=[tokw, XTL, h_rd[bk], MM2LAST])
                                tkmm = p.mark("t")
                                GL = p.op("s", lambda e, f=f, nh=nh, bk=bk:
                                          e.activation(hT[:, f * 1024 + nh * 512:
                                                          f * 1024 + (nh + 1) * 512],
                                                       PB[bk][:, :], AF.Gelu,
                                                       bias=b1all[:, e_ * 32 + f: e_ * 32 + f + 1]),
                                          deps=[tkmm, B1L, MM2LAST], mark=True)
                                h_rd[bk] = GL
                                HDONE = GL
                            w1_war[sl] = tkmm
                            if iw1 + 3 < len(seq_w1):
                                load_w1(iw1 + 3)
                            iw1 += 1
                        for dq in range(4):
                            tokw2, sl2 = w2_tok[iw2]
                            for tcl in range(8):
                                ob = 4 + (tcl % 2)
                                gtc = Tt * 8 + tcl
                                for f in range(32):
                                    p.op("t", lambda e, f=f, tcl=tcl, sl2=sl2, ob=ob:
                                         e.matmul(PB[ob][:, 0:256],
                                                  hT[:, f * 1024 + tcl * 128:
                                                     f * 1024 + tcl * 128 + 128],
                                                  w2b[sl2][:, f * 256:(f + 1) * 256],
                                                  start=(f == 0), stop=(f == 31)),
                                         deps=[tokw2, HDONE, o_ev[tcl % 2],
                                               EPB5X if ob == 5 else None])
                                tk2 = p.mark("t")
                                o_ev[tcl % 2] = p.op(
                                    "v", lambda e, gtc=gtc, dq=dq, e_=e_, ob=ob:
                                    e.scalar_tensor_tensor(
                                        acc[:, gtc * 1024 + dq * 256: gtc * 1024 + (dq + 1) * 256],
                                        PB[ob][:, 0:256],
                                        c_sb[:, gtc * 8 + e_: gtc * 8 + e_ + 1],
                                        acc[:, gtc * 1024 + dq * 256: gtc * 1024 + (dq + 1) * 256],
                                        ALU.mult, ALU.add),
                                    deps=[tk2, ACCL, CCLAST], mark=True)
                            w2_war[sl2] = tk2
                            if iw2 + 2 < len(seq_w2):
                                load_w2(iw2 + 2)
                            iw2 += 1
                        MM2LAST = tk2

                ACCDONE = max(o_ev[0], o_ev[1], key=lambda t: t[1])
                es_f1.close()
                b2_sb = sbf("b2_sb", [8, 1024], BF16)
                p.new_sem("b2sem")
                B2L = p.dma("g", b2_sb[:, :], b2_in, "b2sem",
                            deps=[MM2LAST, ACCDONE, h_rd[0], h_rd[1], h_rd[2], h_rd[3]])

                p.new_sem("osem0")
                p.new_sem("osem1")
                b2done = None
                y2_war = [None]
                for gtc in range(NT):
                    for nh2 in range(2):
                        p.op("t", lambda e, gtc=gtc, nh2=nh2:
                             e.matmul(PB[6 + nh2][:, :],
                                      cT[0:8, gtc * 128:(gtc + 1) * 128],
                                      b2_sb[0:8, nh2 * 512:(nh2 + 1) * 512],
                                      start=True, stop=True),
                             deps=[B2L, CT_LAST, EPB[6], EPB[7], b2done])
                    BTK = p.mark("t")
                    A0 = p.op("v", lambda e, gtc=gtc:
                              e.tensor_tensor(acc[:, gtc * 1024: gtc * 1024 + 512],
                                              acc[:, gtc * 1024: gtc * 1024 + 512],
                                              PB[6][:, :], ALU.add),
                              deps=[BTK, o_ev[0], o_ev[1], ACCDONE], mark=True)
                    A1 = p.op("v", lambda e, gtc=gtc:
                              e.tensor_tensor(acc[:, gtc * 1024 + 512: gtc * 1024 + 1024],
                                              acc[:, gtc * 1024 + 512: gtc * 1024 + 1024],
                                              PB[7][:, :], ALU.add),
                              deps=[BTK], mark=True)
                    b2done = A1
                    NS = p.op("v", lambda e, gtc=gtc:
                              e.reduce_sum(stat[:, 2:3], acc[:, gtc * 1024:(gtc + 1) * 1024],
                                           axis=AX, negate=True),
                              deps=[A0, A1], mark=True)
                    NM = p.op("v", lambda e: e.tensor_scalar(stat[:, 3:4], stat[:, 2:3],
                                                             1.0 / 1024, None, ALU.mult),
                              deps=[NS], mark=True)
                    SQ = p.op("s", lambda e, gtc=gtc:
                              e.activation(hT[:, 0:1024], acc[:, gtc * 1024:(gtc + 1) * 1024],
                                           AF.Square, bias=stat[:, 3:4], accum_out=stat[:, 4:5]),
                              deps=[A0, A1, NM, y2_war[0], MM2LAST], mark=True)
                    y2_war[0] = SQ
                    LNV = p.op("s", lambda e: e.activation(stat[:, 5:6], stat[:, 4:5], AF.Ln,
                                                           scale=1.0 / 1024, bias=eps_c[:, :]),
                               deps=[SQ, EPSC], mark=True)
                    RSD = p.op("s", lambda e: e.activation(stat[:, 6:7], stat[:, 5:6], AF.Exp,
                                                           scale=-0.5),
                               deps=[LNV], mark=True)
                    NMR = p.op("v", lambda e: e.tensor_tensor(stat[:, 7:8], stat[:, 3:4],
                                                              stat[:, 6:7], ALU.mult),
                               deps=[RSD, NM], mark=True)
                    OA = p.op("v", lambda e, gtc=gtc:
                              e.tensor_scalar(acc[:, gtc * 1024:(gtc + 1) * 1024],
                                              acc[:, gtc * 1024:(gtc + 1) * 1024],
                                              stat[:, 6:7], stat[:, 7:8], ALU.mult, ALU.add),
                              deps=[NMR, SQ], mark=True)
                    ST = p.dma("y", out_ap[gtc * 128:(gtc + 1) * 128, :],
                               acc[:, gtc * 1024:(gtc + 1) * 1024],
                               f"osem{gtc % 2}", deps=[OA])

                p.wait("y", ("osem0", p.cnt["osem0"]))
                p.wait("y", ("osem1", p.cnt["osem1"]))
                if DBG:
                    for k in dbg:
                        p.wait("y", dbg[k])

        except _SkipRest:
            pass

        with nc.Block() as block:
            p.realize(block)

    return nc, sorted(dbg.keys())


class _SkipRest(Exception):
    pass



def _build_masks():
    m = np.zeros((128, 4 * 512), dtype=BF16NP)
    for var in range(4):
        par, fb = var % 2, var // 2
        kp = np.arange(128)[:, None]
        col = np.arange(512)[None, :]
        kc, q = col // 128, col % 128
        j = kc * 128 + kp
        i = par * 128 + q
        ok = (j > i) & (j <= i + 256)
        if fb:
            ok = ok & (j >= 256)
        m[:, var * 512:(var + 1) * 512] = ok.astype(BF16NP)
    return m


def _build_latmask(half):
    m = np.zeros((128, NT * 64), dtype=BF16NP)
    for tc in range(NT):
        pcol = np.arange(128)[:, None]
        n_rel = np.arange(64)[None, :]
        s_glob = half * 2048 + tc * 128 + pcol
        n_g = np.where(n_rel < 32, half * 32 + n_rel, (1 - half) * 32 + (n_rel - 32))
        ok = n_g <= (s_glob // 64)
        m[:, tc * 64:(tc + 1) * 64] = ok.astype(BF16NP)
    return m


def kernel(L0_attn_x, L0_attn_wq, L0_attn_wk, L0_attn_wv, L0_attn_wgq, L0_attn_wgk,
           L0_attn_wgv, L0_attn_r_out_w, L0_attn_gate_local_w, L0_attn_gate_remote_w,
           L0_attn_w_out, L0_ffn_router_w, L0_ffn_router_b, L0_ffn_lang_w, L0_ffn_lang_b,
           L0_ffn_w1, L0_ffn_b1, L0_ffn_w2, L0_ffn_b2):
    f32 = lambda a: np.ascontiguousarray(np.asarray(a, dtype=np.float32))
    x = f32(L0_attn_x)
    wq = f32(L0_attn_wq); wk = f32(L0_attn_wk); wv = f32(L0_attn_wv)
    wgq = f32(L0_attn_wgq); wgk = f32(L0_attn_wgk); wgv = f32(L0_attn_wgv)
    ro = f32(L0_attn_r_out_w); glw = f32(L0_attn_gate_local_w)
    grw = f32(L0_attn_gate_remote_w); wo = f32(L0_attn_w_out)
    rw = f32(L0_ffn_router_w); rb = f32(L0_ffn_router_b)
    lw = f32(L0_ffn_lang_w); lb = f32(L0_ffn_lang_b)
    w1 = f32(L0_ffn_w1); b1 = f32(L0_ffn_b1); w2 = f32(L0_ffn_w2); b2 = f32(L0_ffn_b2)

    maskT_all = _build_masks()
    maskT_h1 = maskT_all.copy()
    maskT_h1[:, 2 * 512:4 * 512] = maskT_all[:, 0:2 * 512]
    shared = {
        "ident": np.eye(128, dtype=np.float32),
        "poolmat": np.kron(np.eye(2, dtype=np.float32),
                           np.full((64, 1), 1.0 / 64, np.float32)),

        "glw": glw, "grw": grw,
        "lrw": np.ascontiguousarray(
            np.concatenate([lw.T, rw.T], axis=1).reshape(8, 128, 12)),
        "lrb": np.concatenate([lb, rb])[None, :].astype(np.float32),
        "wq": np.ascontiguousarray(wq.reshape(8, 128, 8, 128).transpose(1, 0, 2, 3).reshape(128, 8192)),
        "wk": np.ascontiguousarray(wk.reshape(8, 128, 8, 128).transpose(1, 0, 2, 3).reshape(128, 8192)),
        "wo": np.ascontiguousarray(wo.reshape(8, 128, 8, 128).transpose(1, 0, 2, 3).reshape(128, 8192)),
        "wv": np.ascontiguousarray(wv.reshape(8, 128, 1024).transpose(1, 0, 2).reshape(128, 8192)),
        "wgq": np.ascontiguousarray(wgq.reshape(8, 128, 4, 128).transpose(1, 0, 2, 3).reshape(128, 4096)),
        "wgk": np.ascontiguousarray(wgk.reshape(8, 128, 4, 128).transpose(1, 0, 2, 3).reshape(128, 4096)),
        "wgv": np.ascontiguousarray(wgv.reshape(8, 128, 512).transpose(1, 0, 2).reshape(128, 4096)),
        "ro": np.ascontiguousarray(ro.reshape(8, 64, 1024).transpose(1, 0, 2).reshape(64, 8192)),
        "b1": np.ascontiguousarray(b1.reshape(8, 32, 128).transpose(0, 2, 1)),
        "b2": b2,
        "w1": np.ascontiguousarray(
            w1.reshape(8, 8, 128, 32, 128).transpose(0, 3, 2, 1, 4).reshape(8, 32, 128, 1024)),
        "w2": np.ascontiguousarray(
            w2.reshape(8, 32, 128, 4, 256).transpose(0, 3, 2, 1, 4).reshape(8, 4, 128, 8192)),
    }
    latmasks = [_build_latmask(0), _build_latmask(1)]

    in_maps = []
    for c in range(8):
        b, half = c // 2, c % 2
        own = x[b, half * 2048:(half + 1) * 2048]
        halo = (x[b, 2048 - 256:2048] if half == 1 else np.zeros((256, D), np.float32))
        other = x[b, (1 - half) * 2048:(2 - half) * 2048]
        x_cb = np.ascontiguousarray(
            np.concatenate([halo, own, other], axis=0).reshape(34, 128, D))
        m = dict(shared)
        m["x_comb"] = x_cb
        m["latmask"] = latmasks[half]
        m["maskT"] = maskT_all if half == 0 else maskT_h1
        in_maps.append(m)

    nc, dbg_names = build_program()
    res = run_bass_kernel_spmd(nc, in_maps, core_ids=list(range(8)))

    out = np.empty((4, 4096, D), dtype=np.float32)
    for c in range(8):
        b, half = c // 2, c % 2
        out[b, half * 2048:(half + 1) * 2048] = res.results[c]["out"]
    if DBG:
        kernel._dbg = [{k: r.get("dbg_" + k) for k in dbg_names} for r in res.results]
    return out


if __name__ == "__main__":
    nc, names = build_program()
    print("built ok; dbg:", names)


# version 10
# speedup vs baseline: 1.0818x; 1.0818x over previous
import os
from contextlib import ExitStack

import numpy as np
import ml_dtypes

import concourse.bass as bass
import concourse.mybir as mybir
from concourse.bass_utils import run_bass_kernel_spmd

F32 = mybir.dt.float32
BF16 = mybir.dt.bfloat16
AF = mybir.ActivationFunctionType
ALU = mybir.AluOpType
AX = mybir.AxisListType.X
BF16NP = ml_dtypes.bfloat16

D = 1024
NT = 16
NH = 16
NLH = 8
NE = 8
DBG = os.environ.get("KDBG", "") == "1"
KSTAGE = int(os.environ.get("KSTAGE", "3"))
KTC = int(os.environ.get("KTC", "16"))
KPART = int(os.environ.get("KPART", "9"))
KLAT = int(os.environ.get("KLAT", "9"))

ENGS = ("t", "v", "s", "g", "y")


class P:

    def __init__(self, nc, es):
        self.nc = nc
        self.es = es
        self.ops = {e: [] for e in ENGS}
        self.sem = {}
        self.cnt = {}
        self.hw = {e: {} for e in ENGS}
        for e in ENGS:
            self.new_sem("p" + e)

    def new_sem(self, name):
        self.sem[name] = self.es.enter_context(self.nc.semaphore(name))
        self.cnt[name] = 0
        return name

    def wait(self, eng, tok):
        if tok is None:
            return
        name, val = tok
        if self.hw[eng].get(name, 0) >= val:
            return
        self.hw[eng][name] = val
        h = self.sem[name]
        self.ops[eng].append(lambda e, h=h, v=val: e.wait_ge(h, v))

    def op(self, eng, fn, deps=(), mark=False):
        for t in deps:
            self.wait(eng, t)
        if mark:
            name = "p" + eng
            self.cnt[name] += 1
            tok = (name, self.cnt[name])
            h = self.sem[name]
            self.ops[eng].append(lambda e, fn=fn, h=h: fn(e).then_inc(h, 1))
            return tok
        self.ops[eng].append(fn)
        return None

    def mark(self, eng):
        name = "p" + eng
        self.cnt[name] += 1
        tok = (name, self.cnt[name])
        h = self.sem[name]
        lst = self.ops[eng]
        assert lst, "mark with no prior op"
        prev = lst.pop()
        def wrapped(e, prev=prev, h=h):
            ins = prev(e)
            ins.then_inc(h, 1)
            return ins
        lst.append(wrapped)
        return tok

    def dma(self, eng, out_ap, in_ap, sem_name, deps=(), **kw):
        for t in deps:
            self.wait(eng, t)
        self.cnt[sem_name] += 16
        tok = (sem_name, self.cnt[sem_name])
        h = self.sem[sem_name]
        self.ops[eng].append(
            lambda e, o=out_ap, i=in_ap, h=h, kw=kw: e.dma_start(o, i, **kw).then_inc(h, 16))
        return tok

    def realize(self, block):
        def body(lst):
            return lambda e: [fn(e) for fn in lst]
        block.tensor(body(self.ops["t"]))
        block.vector(body(self.ops["v"]))
        block.scalar(body(self.ops["s"]))
        block.gpsimd(body(self.ops["g"]))
        block.sync(body(self.ops["y"]))


def build_program():
    nc = bass.Bass("TRN2", target_bir_lowering=False)
    es = ExitStack()

    def din(name, shape, dt=F32):
        return nc.dram_tensor(name, shape, dt, kind="ExternalInput").ap()

    x_comb = din("x_comb", [34, 128, D])
    maskT_in = din("maskT", [128, 4 * 512], BF16)
    latmask_in = din("latmask", [128, NT * 64], BF16)
    ident_in = din("ident", [128, 128])
    poolmat_in = din("poolmat", [128, 2])
    glw_in = din("glw", [NH, 64])
    grw_in = din("grw", [NH, 64])
    lrw_in = din("lrw", [8, 128, 12])
    lrb_in = din("lrb", [1, 12])
    wq_in = din("wq", [128, 8192])
    wk_in = din("wk", [128, 8192])
    wv_in = din("wv", [128, 8192])
    wo_in = din("wo", [128, 8192])
    wgq_in = din("wgq", [128, 4096])
    wgk_in = din("wgk", [128, 4096])
    wgv_in = din("wgv", [128, 4096])
    ro_in = din("ro", [64, 8192])
    b1_in = din("b1", [NE, 128, 32])
    b2_in = din("b2", [NE, D])
    w1_in = din("w1", [NE, 32, 128, 1024])
    w2_in = din("w2", [NE, 4, 128, 8192])

    out_ap = nc.dram_tensor("out", [2048, D], F32, kind="ExternalOutput").ap()
    x1s = nc.dram_tensor("x1s", [NT, 128, D], F32).ap()
    x1Ts = nc.dram_tensor("x1Ts", [8, 128, 2048], BF16).ap()

    dbg = {}

    with es:
        p = P(nc, es)

        def sb(name, shape, dt):
            return es.enter_context(nc.sbuf_tensor(name, shape, dt))

        def dump(name, ap, shape, dt, deps):
            if not DBG:
                return None
            t = nc.dram_tensor("dbg_" + name, shape, dt, kind="ExternalOutput").ap()
            sm = p.new_sem("dbg_" + name)
            dbg[name] = p.dma("y", t, ap, sm, deps=deps)
            return dbg[name]

        id_f = sb("id_f", [128, 128], F32)
        ones_f = sb("ones_f", [1, 128], F32)
        poolmat = sb("poolmat_s", [128, 2], F32)
        Wg = sb("Wg", [128, 8 * 32], BF16)
        c_sb = sb("c_sb", [128, NT * 8], F32)
        cT = sb("cT", [8, NT * 128], BF16)
        lrw = sb("lrw_s", [128, 8 * 12], F32)
        lrb = sb("lrb_s", [1, 12], F32)
        stat = sb("stat", [128, 48], F32)
        lp_pad = sb("lp_pad", [128, 5], F32)
        rl_sb = sb("rl_sb", [128, 8], F32)
        p8_sb = sb("p8_sb", [128, 8], F32)
        pm_sb = sb("pm_sb", [128, 8], F32)
        ge_sb = sb("ge_sb", [128, 8], F32)

        PB = [es.enter_context(nc.psum_tensor(f"pb{i}", [128, 512], F32)) for i in range(8)]

        p.new_sem("csem")
        p.dma("y", id_f[:, :], ident_in, "csem")
        p.dma("y", poolmat[:, :], poolmat_in, "csem")
        p.dma("y", lrw[:, :].rearrange("p (k c) -> p k c", k=8),
              lrw_in.rearrange("k p c -> p k c"), "csem")
        p.dma("y", lrb[:, :], lrb_in, "csem")
        CONST = ("csem", p.cnt["csem"])
        IDB = None

        eps_c = sb("eps_c", [128, 1], F32)
        EPSC = p.op("v", lambda e: e.memset(eps_c[:, :], 1e-5), mark=True)
        t_wg0 = p.op("v", lambda e: e.memset(Wg[:, :], 0.0), mark=True)
        ONESF = p.op("v", lambda e: e.memset(ones_f[:, :], 1.0), mark=True)
        LPP0 = p.op("v", lambda e: e.memset(lp_pad[:, 4:5], 0.0), mark=True)
        p.new_sem("wgsem")
        for h in range(NH):
            dc, po = h // 2, 64 * (h % 2)
            for gi, src in ((0, glw_in), (1, grw_in)):
                p.dma("g", Wg[po:po + 64, dc * 32 + 16 * gi + h: dc * 32 + 16 * gi + h + 1],
                      src[h:h + 1, :].rearrange("a b -> b a"), "wgsem",
                      deps=(t_wg0,), allow_slow_non_contiguous=True)
        WGTOK = ("wgsem", p.cnt["wgsem"])

        try:
            with ExitStack() as es_att:
                def sba(name, shape, dt):
                    return es_att.enter_context(nc.sbuf_tensor(name, shape, dt))

                xT = sba("xT", [128, 8 * 2304], BF16)
                QT = sba("QT", [128, 8 * 2048], BF16)
                GQT = sba("GQT", [128, 4 * 2048], BF16)
                GKT = sba("GKT", [128, 8 * 64], BF16)
                GV2z = sba("GV2z", [128, 512], BF16)
                glgr = sba("glgr", [128, NT * 32], F32)

                es_b = ExitStack()
                def sbb(name, shape, dt):
                    return es_b.enter_context(nc.sbuf_tensor(name, shape, dt))
                xpT = sbb("xpT", [128, 8 * 64], BF16)
                x_in = [sbb(f"x_in{i}", [128, D], F32) for i in range(4)]
                wq_t = sbb("wq_t", [128, 8192], BF16)
                wgq_t = sbb("wgq_t", [128, 4096], BF16)
                wgk_t = sbb("wgk_t", [128, 4096], BF16)
                wgv_t = sbb("wgv_t", [128, 4096], BF16)
                GV2 = sbb("GV2", [64, 512], BF16)

                def wload(name, dst, src, deps=()):
                    p.new_sem("ws_" + name)
                    return p.dma("g", dst, src, "ws_" + name, deps=deps)

                WGK = wload("wgk", wgk_t[:, :], wgk_in)
                WGV = wload("wgv", wgv_t[:, :], wgv_in)
                WQ = wload("wq", wq_t[:, :], wq_in)
                WGQ = wload("wgq", wgq_t[:, :], wgq_in)

                for i in range(4):
                    p.new_sem(f"xsem{i}")
                x_slot_rd = [None] * 4
                tev = [None, None]
                for i in range(34):
                    sl = i % 4
                    deps = [x_slot_rd[sl]] if x_slot_rd[sl] else []
                    XL = p.dma("y", x_in[sl][:, :], x_comb[i, :, :], f"xsem{sl}", deps=deps)
                    if i < 18:
                        for j in range(2):
                            bk = j % 2
                            for dc in range(4 * j, 4 * j + 4):
                                p.op("t", lambda e, b=PB[bk], dc=dc, sl=sl:
                                     e.transpose(b[:, (dc % 4) * 128:(dc % 4) * 128 + 128],
                                                 x_in[sl][:, dc * 128:(dc + 1) * 128], id_f[:, :]),
                                     deps=[XL, CONST, tev[bk]])
                            tk = p.mark("t")
                            dst = (xT[:, :].rearrange("p (dc c) -> p dc c", c=2304)
                                   [:, 4 * j:4 * j + 4, i * 128:(i + 1) * 128])
                            src = PB[bk][:, :].rearrange("p (a b) -> p a b", b=128)
                            tev[bk] = p.op("v", lambda e, d=dst, s=src: e.tensor_copy(d, s),
                                           deps=[tk], mark=True)
                    if i >= 2:
                        j2 = i - 2
                        for dc in range(8):
                            p.op("t", lambda e, sl=sl, dc=dc, j2=j2:
                                 e.matmul(PB[7][:, dc * 64 + 2 * j2: dc * 64 + 2 * j2 + 2],
                                          x_in[sl][:, dc * 128:(dc + 1) * 128],
                                          poolmat[:, :], start=True, stop=True),
                                 deps=[XL, CONST])
                        x_slot_rd[sl] = p.mark("t")
                    else:
                        x_slot_rd[sl] = tk
                POOLT = x_slot_rd[33 % 4]
                XTD = [tev[0], tev[1]]
                XP = p.op("v", lambda e: e.tensor_copy(xpT[:, :], PB[7][:, :]),
                          deps=[POOLT], mark=True)
                dump("xpT", xpT[:, :], [128, 512], BF16, [XP])

                if KSTAGE < 1:
                    es_b.close()
                    es_att.close()
                    raise _SkipRest()
                for gc in range(4):
                    for kc in range(8):
                        p.op("t", lambda e, gc=gc, kc=kc:
                             e.matmul(PB[5][:, gc * 64:(gc + 1) * 64],
                                      wgk_t[:, kc * 512 + gc * 128: kc * 512 + gc * 128 + 128],
                                      xpT[:, kc * 64:(kc + 1) * 64],
                                      start=(kc == 0), stop=(kc == 7)),
                             deps=[XP, WGK])
                GKMM = p.mark("t")
                GKZ0 = p.op("v", lambda e: e.memset(GKT[:, :], 0.0), mark=True)
                for lh in range(8):
                    GKEV = p.op("v", lambda e, lh=lh:
                                e.tensor_copy(GKT[64 * (lh % 2):64 * (lh % 2) + 64,
                                                  lh * 64:(lh + 1) * 64],
                                              PB[5][64 * (lh % 2):64 * (lh % 2) + 64,
                                                    (lh // 2) * 64:(lh // 2) * 64 + 64]),
                                deps=[GKMM, GKZ0], mark=True)
                for kc in range(8):
                    p.op("t", lambda e, kc=kc:
                         e.matmul(PB[6][0:64, :], xpT[:, kc * 64:(kc + 1) * 64],
                                  wgv_t[:, kc * 512:(kc + 1) * 512],
                                  start=(kc == 0), stop=(kc == 7)),
                         deps=[XP, WGV])
                GVMM = p.mark("t")
                GVZ0 = p.op("v", lambda e: e.memset(GV2z[:, :], 0.0), mark=True)
                GVEV = p.op("v", lambda e: e.tensor_copy(GV2[0:64, :], PB[6][0:64, :]),
                            deps=[GVMM], mark=True)
                p.new_sem("gv2sem")
                ev_dst = GV2z[0:64, :].rearrange("p (l c) -> p l c", c=128)[:, :, 0:64]
                ev_src = GV2[0:64, :].rearrange("p (l c) -> p l c", c=128)[:, :, 0:64]
                od_dst = GV2z[64:128, :].rearrange("p (l c) -> p l c", c=128)[:, :, 64:128]
                od_src = GV2[0:64, :].rearrange("p (l c) -> p l c", c=128)[:, :, 64:128]
                p.dma("y", ev_dst, ev_src, "gv2sem", deps=[GVEV, GVZ0])
                GV2D = p.dma("y", od_dst, od_src, "gv2sem", deps=[GVEV, GVZ0])
                dump("GKT", GKT[:, :], [128, 512], BF16, [GKEV])
                dump("GV2", GV2z[:, :], [128, 512], BF16, [GV2D])

                qev = [None, None]
                for dc in range(8):
                    for n4 in range(4):
                        bk = (dc * 4 + n4) % 2
                        for kc in range(8):
                            p.op("t", lambda e, dc=dc, n4=n4, kc=kc, b=PB[2 + bk]:
                                 e.matmul(b[:, :],
                                          wq_t[:, kc * 1024 + dc * 128: kc * 1024 + dc * 128 + 128],
                                          xT[:, kc * 2304 + 256 + n4 * 512: kc * 2304 + 256 + (n4 + 1) * 512],
                                          start=(kc == 0), stop=(kc == 7)),
                                 deps=[XTD[0], XTD[1], WQ, qev[bk]])
                        tk = p.mark("t")
                        qev[bk] = p.op("v", lambda e, dc=dc, n4=n4, b=PB[2 + bk]:
                                       e.tensor_copy(QT[:, dc * 2048 + n4 * 512: dc * 2048 + (n4 + 1) * 512],
                                                     b[:, :]),
                                       deps=[tk], mark=True)
                QDONE = [qev[0], qev[1]]
                for dc in range(4):
                    for n4 in range(4):
                        bk = (dc * 4 + n4) % 2
                        for kc in range(8):
                            p.op("t", lambda e, dc=dc, n4=n4, kc=kc, b=PB[2 + bk]:
                                 e.matmul(b[:, :],
                                          wgq_t[:, kc * 512 + dc * 128: kc * 512 + dc * 128 + 128],
                                          xT[:, kc * 2304 + 256 + n4 * 512: kc * 2304 + 256 + (n4 + 1) * 512],
                                          start=(kc == 0), stop=(kc == 7)),
                                 deps=[XTD[0], XTD[1], WGQ, qev[bk]])
                        tk = p.mark("t")
                        qev[bk] = p.op("v", lambda e, dc=dc, n4=n4, b=PB[2 + bk]:
                                       e.tensor_copy(GQT[:, dc * 2048 + n4 * 512: dc * 2048 + (n4 + 1) * 512],
                                                     b[:, :]),
                                       deps=[tk], mark=True)
                GQDONE = [qev[0], qev[1]]
                dump("QT", QT[:, 0:2048], [128, 2048], BF16, QDONE)
                for tc in range(NT):
                    for kc in range(8):
                        p.op("t", lambda e, tc=tc, kc=kc:
                             e.matmul(PB[4][:, tc * 32:(tc + 1) * 32],
                                      QT[:, kc * 2048 + tc * 128: kc * 2048 + tc * 128 + 128],
                                      Wg[:, kc * 32:(kc + 1) * 32],
                                      start=(kc == 0), stop=(kc == 7)),
                             deps=[QDONE[0], QDONE[1], WGTOK])
                GMM = p.mark("t")
                GSIG = p.op("s", lambda e: e.activation(glgr[:, :], PB[4][:, :], AF.Sigmoid),
                            deps=[GMM], mark=True)
                dump("glgr", glgr[:, :], [128, 512], F32, [GSIG])

                if KSTAGE < 2:
                    es_b.close()
                    es_att.close()
                    raise _SkipRest()
                es_b.close()
                es_e = ExitStack()
                def sbe(name, shape, dt):
                    return es_e.enter_context(nc.sbuf_tensor(name, shape, dt))
                wk_t = sbe("wk_t", [128, 8192], BF16)
                wv_t = sbe("wv_t", [128, 8192], BF16)
                wo_t = sbe("wo_t", [128, 8192], BF16)
                ro8 = sbe("ro8", [64, 8192], BF16)
                Kroll = sbe("Kroll", [128, 16 * 512], BF16)
                Vroll = [sbe(f"Vroll{i}", [128, 16 * 65], BF16) for i in range(4)]
                Pt = [sbe(f"Pt{i}", [128, 512], BF16) for i in range(3)]
                lp_tok = sbe("lp_tok", [128, 512], BF16)
                latT8 = sbe("latT8", [64, 1024], BF16)
                comb = sbe("comb", [128, D], BF16)
                x1_tc = sbe("x1_tc", [128, D], F32)
                x1T_tc = sbe("x1T_tc", [128, D], BF16)
                x1Tf = sbe("x1Tf", [128, D], F32)
                maskT = sbe("maskT_s", [128, 4 * 512], BF16)
                latmask = sbe("latmask_s", [128, NT * 64], BF16)
                lpn = lp_tok
                lptn = lp_tok

                TCD = tk
                DBGT = max((dbg[k] for k in dbg), key=lambda t: 0) if False else None
                if DBG:
                    DBGT = None
                    dbg_gates = [dbg[k] for k in dbg]
                else:
                    dbg_gates = []
                GVDMA = ("gv2sem", p.cnt["gv2sem"])
                WK = wload("wk", wk_t[:, :], wk_in, deps=[TCD, GVDMA] + dbg_gates)
                WV = wload("wv", wv_t[:, :], wv_in, deps=[TCD, GVDMA] + dbg_gates)
                WO = wload("wo", wo_t[:, :], wo_in, deps=[TCD, GVDMA] + dbg_gates)
                RO = wload("ro", ro8[:, :], ro_in, deps=[TCD, GVDMA] + dbg_gates)
                p.new_sem("msksem")
                MASKS = p.dma("y", maskT[:, :], maskT_in, "msksem", deps=[TCD, GVDMA] + dbg_gates)
                MASKS = p.dma("y", latmask[:, :], latmask_in, "msksem", deps=[TCD, GVDMA] + dbg_gates)
                MASKS = ("msksem", p.cnt["msksem"])
                VR1 = [p.op("v", lambda e, t=t: e.memset(t[:, :], 1.0),
                            deps=[TCD, GVDMA] + dbg_gates, mark=True)
                       for t in Vroll]
                KR0 = p.op("v", lambda e: e.memset(Kroll[:, :], 0.0),
                           deps=[TCD, GVDMA] + dbg_gates, mark=True)

                kroll_rd = [[KR0], [KR0]]
                kroll_wr = [KR0, KR0]
                vroll_rd = [VR1[0], VR1[1], VR1[2], VR1[3]]
                vroll_wr = [VR1[0], VR1[1], VR1[2], VR1[3]]
                pt_rd = [None] * 3
                comb_rd = None
                dbg_comb = None
                x1war = []
                x1T_rd = None
                x1Tf_rd = None
                sq_prev = None
                p.new_sem("xresem")
                p.new_sem("x1sem")
                p.new_sem("x1tsem0")
                p.new_sem("x1tsem1")

                st_ev = [GKEV, None, None]
                STB = [0, 1, 5]
                loc_ev = [GSIG] * 4
                lsc_ev = GSIG
                lat_ev = [GVEV, XP]
                lpt_ev = GKEV
                pb5_ex = GKEV
                rem_rd = [GQDONE[0], GQDONE[1]]
                wo_rd = [GQDONE[0], GQDONE[1]]
                pb23_rd = [GQDONE[0], GQDONE[1]]
                combT_ev = GVEV
                rtr_rd = [GKEV, GKEV]
                kv_ev = [None, None, GQDONE[0], GQDONE[1]]
                CC = None
                cc_toks = []
                LEXP = None
                pt_dump = None
                dbg_kroll = None
                LATMM_prev = None
                REMMM_prev = None
                COMBDONE_prev = None

                S = {"lsc_ev": GSIG, "lexp": {}, "latmm": None}

                def emit_roll(tc2):
                    if tc2 % 2 != 0:
                        return
                    for b in ([-1, 0] if tc2 == 0 else [tc2 // 2]):
                        slot = b % 2
                        for mc in range(8):
                            bk = mc % 2
                            for kc in range(8):
                                p.op("t", lambda e, mc=mc, kc=kc, b=b, bkk=PB[bk]:
                                     e.matmul(bkk[:, 0:256],
                                              wk_t[:, kc * 1024 + mc * 128: kc * 1024 + mc * 128 + 128],
                                              xT[:, kc * 2304 + 256 + b * 256: kc * 2304 + 256 + b * 256 + 256],
                                              start=(kc == 0), stop=(kc == 7)),
                                     deps=[XTD[0], XTD[1], WK, st_ev[bk], kv_ev[bk]])
                            tk = p.mark("t")
                            p.op("v", lambda e, mc=mc, slot=slot, bkk=PB[bk]:
                                 e.tensor_copy(Kroll[0:64, (2 * mc) * 512 + slot * 256:
                                                     (2 * mc) * 512 + slot * 256 + 256],
                                               bkk[0:64, 0:256]),
                                 deps=[tk] + kroll_rd[slot])
                            kv_ev[bk] = p.op(
                                "v", lambda e, mc=mc, slot=slot, bkk=PB[bk]:
                                e.tensor_copy(Kroll[64:128, (2 * mc + 1) * 512 + slot * 256:
                                                    (2 * mc + 1) * 512 + slot * 256 + 256],
                                              bkk[64:128, 0:256]),
                                deps=[tk], mark=True)
                        kroll_wr[slot] = max(kv_ev[0], kv_ev[1], key=lambda t: t[1])
                        for tt in range(2):
                            vt = slot * 2 + tt
                            for nh in range(2):
                                bk = 2 + (tt * 2 + nh) % 2
                                for kc in range(8):
                                    p.op("t", lambda e, tt=tt, nh=nh, kc=kc, b=b, bkk=PB[bk]:
                                         e.matmul(bkk[:, :],
                                                  xT[:, kc * 2304 + 256 + b * 256 + tt * 128:
                                                     kc * 2304 + 256 + b * 256 + tt * 128 + 128],
                                                  wv_t[:, kc * 1024 + nh * 512: kc * 1024 + (nh + 1) * 512],
                                                  start=(kc == 0), stop=(kc == 7)),
                                         deps=[WV, rem_rd[bk - 2], pb23_rd[bk - 2], kv_ev[bk]])
                                tk = p.mark("t")
                                dst = (Vroll[vt][:, :].rearrange("p (h c) -> p h c", c=65)
                                       [:, nh * 8:(nh + 1) * 8, 0:64])
                                src = PB[bk][:, :].rearrange("p (h c) -> p h c", c=64)
                                kv_ev[bk] = p.op("v", lambda e, d=dst, s=src: e.tensor_copy(d, s),
                                                 deps=[tk, vroll_rd[vt]], mark=True)
                            vroll_wr[vt] = max(kv_ev[2], kv_ev[3], key=lambda t: t[1])

                def emit_lsc(tc2):
                    for lh in range(NLH):
                        p.op("t", lambda e, lh=lh, tc=tc2:
                             e.matmul(PB[4][:, lh * 64:(lh + 1) * 64],
                                      GQT[:, (lh // 2) * 2048 + tc * 128:
                                          (lh // 2) * 2048 + tc * 128 + 128],
                                      GKT[:, lh * 64:(lh + 1) * 64],
                                      start=True, stop=True),
                             deps=[GQDONE[0], GQDONE[1], GKEV, S["lsc_ev"]] + loc_ev)
                    LSCMM = p.mark("t")
                    lx = p.op("s", lambda e:
                              e.activation(lp_tok[:, :], PB[4][:, :], AF.Exp, scale=0.125),
                              deps=[LSCMM, S["lexp"].get(tc2 - 1), S["latmm"]], mark=True)
                    S["lexp"][tc2] = lx
                    S["lsc_ev"] = lx

                emit_roll(0)
                emit_lsc(0)
                dbg_kroll = dump("Kroll0", Kroll[:, :], [128, 8192], BF16,
                                 [kroll_wr[0], kroll_wr[1]])

                for tc in range(min(NT, KTC)):
                    n, par = tc // 2, tc % 2
                    cslot = tc % 2
                    LEXP = S["lexp"][tc]
                    lsc_ev = LEXP
                    lm = []
                    for lh in range(NLH):
                        lm.append(p.op("v", lambda e, lh=lh, tc=tc:
                                       e.tensor_tensor(lp_tok[:, lh * 64:(lh + 1) * 64],
                                                       lp_tok[:, lh * 64:(lh + 1) * 64],
                                                       latmask[:, tc * 64:(tc + 1) * 64], ALU.mult),
                                       deps=[LEXP, MASKS], mark=True))
                    sums = []
                    for lh in range(NLH):
                        sums.append(p.op("v", lambda e, lh=lh:
                                         e.reduce_sum(stat[:, 8 + lh:9 + lh],
                                                      lp_tok[:, lh * 64:(lh + 1) * 64], axis=AX),
                                         deps=[lm[lh]], mark=True))
                    if KLAT < 1:
                        continue
                    RCP = p.op("v", lambda e: e.reciprocal(stat[:, 16:24], stat[:, 8:16]),
                               deps=sums, mark=True)
                    nrm = []
                    for lh in range(NLH):
                        nrm.append(p.op("v", lambda e, lh=lh:
                                        e.tensor_scalar(x1Tf[:, lh * 64:(lh + 1) * 64],
                                                        lp_tok[:, lh * 64:(lh + 1) * 64],
                                                        stat[:, 16 + lh:17 + lh], None, ALU.mult),
                                        deps=[RCP, x1Tf_rd, sq_prev], mark=True))
                    if KLAT < 2:
                        continue
                    lpdump = dump("lp_tok0", x1Tf[:, 0:512], [128, 512], F32, nrm) \
                        if tc == 0 else None
                    for j in range(4):
                        p.op("t", lambda e, j=j:
                             e.transpose(PB[5][:, j * 128:(j + 1) * 128],
                                         x1Tf[:, j * 128:(j + 1) * 128], id_f[:, :]),
                             deps=[nrm[2 * j], nrm[2 * j + 1], CONST, lpt_ev, pb5_ex,
                                   rtr_rd[0], rtr_rd[1], lpdump])
                    TTS = p.mark("t")
                    LPTEV = p.op("v", lambda e: e.tensor_copy(lptn[:, :], PB[5][:, :]),
                                 deps=[TTS] + nrm, mark=True)
                    lpt_ev = LPTEV
                    if KLAT < 3:
                        continue
                    for lh in range(NLH):
                        p.op("t", lambda e, lh=lh:
                             e.matmul(PB[6 + lh // 4][0:64, (lh % 4) * 128:(lh % 4) * 128 + 128],
                                      GV2z[:, 64 * lh: 64 * lh + 64],
                                      lptn[:, (lh // 2) * 128:(lh // 2) * 128 + 128],
                                      start=True, stop=True),
                             deps=[LPTEV, GV2D, lat_ev[lh // 4], combT_ev, COMBDONE_prev])
                    LATMM = p.mark("t")
                    LATMM_prev = LATMM
                    S["latmm"] = LATMM
                    LATEV0 = p.op("v", lambda e: e.tensor_copy(latT8[0:64, 0:512], PB[6][0:64, 0:512]),
                                  deps=[LATMM, REMMM_prev], mark=True)
                    LATEV1 = p.op("v", lambda e: e.tensor_copy(latT8[0:64, 512:1024], PB[7][0:64, 0:512]),
                                  deps=[LATMM], mark=True)
                    lat_ev = [LATEV0, LATEV1]
                    if KLAT < 4:
                        continue
                    for lh in range(NLH):
                        for nh2 in range(2):
                            p.op("t", lambda e, lh=lh, nh2=nh2:
                                 e.matmul(PB[2 + nh2][:, :],
                                          latT8[0:64, lh * 128:(lh + 1) * 128],
                                          ro8[0:64, lh * 1024 + nh2 * 512: lh * 1024 + (nh2 + 1) * 512],
                                          start=(lh == 0), stop=(lh == 7)),
                                 deps=[lat_ev[lh // 4], RO, rem_rd[nh2], pb23_rd[nh2], kv_ev[2 + nh2]])
                    REMMM = p.mark("t")
                    REMMM_prev = REMMM
                    if KLAT < 5:
                        continue
                    rg = []
                    for h in range(NH):
                        rg.append(p.op("v", lambda e, h=h, tc=tc:
                                       e.tensor_scalar(comb[:, h * 64:(h + 1) * 64],
                                                       PB[2 + h // 8][:, (h % 8) * 64:(h % 8) * 64 + 64],
                                                       glgr[:, tc * 32 + 16 + h: tc * 32 + 17 + h],
                                                       None, ALU.mult),
                                       deps=[REMMM, GSIG, comb_rd, dbg_comb],
                                       mark=True))
                    rem_rd[:] = [rg[7], rg[15]]
                    if tc == 0:
                        dtk = dump("comb_rem0", comb[:, :], [128, 1024], BF16, rg)
                        if dtk:
                            rg = [max(t, dtk, key=lambda z: (z[0] != "pv", )) if False else t
                                  for t in rg]
                            dbg_comb = dtk
                            rg = [dtk if False else t for t in rg]
                            rg_gate = dtk
                        else:
                            rg_gate = None
                    else:
                        rg_gate = None

                    if tc == 0:
                        dtk = dump("latT80", latT8[0:64, :], [64, 1024], BF16, lat_ev)
                        if dtk:
                            REMMM_prev = dtk

                    if KPART < 2:
                        continue
                    fin_toks = []
                    sm = {}

                    ACT = (0, 1, 2) if par == 0 else (1, 2, 3)
                    cbase = 0 if par == 0 else 128

                    def emit_st(h):
                        i3 = h % 3
                        bkid = STB[i3]
                        extra = []
                        if bkid in (0, 1):
                            extra = [kv_ev[bkid]]
                        else:
                            extra = [lpt_ev, rtr_rd[0], rtr_rd[1]]
                        for kc in ACT:
                            bb = n - 1 + kc // 2
                            kslot = bb % 2
                            p.op("t", lambda e, h=h, kc=kc, kslot=kslot, tc=tc, bkk=PB[bkid]:
                                 e.matmul(bkk[:, kc * 128:(kc + 1) * 128],
                                          Kroll[:, h * 512 + kslot * 256 + (kc % 2) * 128:
                                                h * 512 + kslot * 256 + (kc % 2) * 128 + 128],
                                          QT[:, (h // 2) * 2048 + tc * 128:
                                             (h // 2) * 2048 + tc * 128 + 128],
                                          start=True, stop=True),
                                 deps=[kroll_wr[kslot], QDONE[0], QDONE[1], st_ev[i3]] + extra)
                        return p.mark("t")

                    def emit_sm(h, stmm):
                        nonlocal pb5_ex
                        i3 = h % 3
                        EX = p.op("s", lambda e, i3=i3, cb=cbase:
                                  e.activation(Pt[i3][:, cb:cb + 384],
                                               PB[STB[i3]][:, cb:cb + 384], AF.Exp, scale=0.125),
                                  deps=[stmm, pt_rd[i3], pt_dump if i3 == 0 else None], mark=True)
                        if STB[i3] == 5:
                            pb5_ex = EX
                        var = par + 2 * (1 if n == 0 else 0)
                        MK = p.op("v", lambda e, i3=i3, var=var, cb=cbase:
                                  e.tensor_tensor(Pt[i3][:, cb:cb + 384],
                                                  Pt[i3][:, cb:cb + 384],
                                                  maskT[:, var * 512 + cb: var * 512 + cb + 384],
                                                  ALU.mult),
                                  deps=[EX, MASKS], mark=True)
                        return EX, MK

                    def emit_pv(h):
                        nonlocal pt_dump
                        if tc == 0 and h == 0 and pt_dump is None:
                            pt_dump = dump("Pt00", Pt[0][:, 0:384], [128, 384], BF16, [sm[0][1]])
                            if pt_dump:
                                pt_rd[0] = pt_dump
                        i3 = h % 3
                        i4 = h % 4
                        lb = 4 if h % 2 == 0 else 6
                        col = 128 * ((h // 2) % 2)
                        mk = sm[h][1]
                        for kc in ACT:
                            bb = n - 1 + kc // 2
                            vt = (bb % 2) * 2 + (kc % 2)
                            p.op("t", lambda e, h=h, kc=kc, vt=vt, i3=i3, lb=lb, col=col,
                                 s_=(kc == ACT[0]), t_=(kc == ACT[-1]):
                                 e.matmul(PB[lb][:, col: col + 65],
                                          Pt[i3][:, kc * 128:(kc + 1) * 128],
                                          Vroll[vt][:, h * 65: h * 65 + 65],
                                          start=s_, stop=t_),
                                 deps=[mk, vroll_wr[vt], loc_ev[h % 2], lsc_ev,
                                       lat_ev[0] if lb == 6 else None])
                        tk = p.mark("t")
                        st_ev[i3] = tk
                        pt_rd[i3] = tk
                        return tk

                    def emit_fin(h, pvtk):
                        i4 = h % 4
                        lb = 4 if h % 2 == 0 else 6
                        col = 128 * ((h // 2) % 2)
                        R = p.op("v", lambda e, lb=lb, col=col:
                                 e.reciprocal(stat[:, 0:1], PB[lb][:, col + 64: col + 65]),
                                 deps=[pvtk], mark=True)
                        G = p.op("v", lambda e, h=h, tc=tc:
                                 e.tensor_tensor(stat[:, 1:2], glgr[:, tc * 32 + h: tc * 32 + h + 1],
                                                 stat[:, 0:1], ALU.mult),
                                 deps=[R, GSIG], mark=True)
                        A = p.op("v", lambda e, h=h, lb=lb, col=col:
                                 e.scalar_tensor_tensor(comb[:, h * 64:(h + 1) * 64],
                                                        PB[lb][:, col: col + 64],
                                                        stat[:, 1:2],
                                                        comb[:, h * 64:(h + 1) * 64],
                                                        ALU.mult, ALU.add),
                                 deps=[G, rg[h], rg_gate], mark=True)
                        loc_ev[h % 2] = A
                        fin_toks.append(A)

                    PIPE = int(os.environ.get('KPIPE', '2'))
                    last_pv = None
                    for h in range(NH + PIPE):
                        if h < NH:
                            stm = emit_st(h)
                            sm[h] = (None, None)
                            sm[h] = emit_sm(h, stm)
                        if h >= PIPE:
                            hh = h - PIPE
                            last_pv = emit_pv(hh)
                            emit_fin(hh, last_pv)
                    COMBDONE = fin_toks[-1]
                    COMBDONE_prev = COMBDONE
                    kroll_rd[:] = [[last_pv, dbg_kroll], [last_pv, dbg_kroll]]
                    vroll_rd[:] = [last_pv] * 4

                    if KPART < 3:
                        continue
                    CB0 = p.op("v", lambda e: e.tensor_copy(x1Tf[:, 0:512], comb[:, 0:512]),
                               deps=[COMBDONE, TTS, x1Tf_rd, sq_prev], mark=True)
                    CB1 = p.op("v", lambda e: e.tensor_copy(x1Tf[:, 512:1024], comb[:, 512:1024]),
                               deps=[COMBDONE], mark=True)
                    ctev = []
                    for rnd in range(2):
                        for q4 in range(4):
                            dc = rnd * 4 + q4
                            p.op("t", lambda e, dc=dc, q4=q4:
                                 e.transpose(PB[6][:, q4 * 128:(q4 + 1) * 128],
                                             x1Tf[:, dc * 128:(dc + 1) * 128], id_f[:, :]),
                                 deps=[CB0, CB1, CONST, lat_ev[0], combT_ev])
                        tk = p.mark("t")
                        cte = p.op("v", lambda e, rnd=rnd:
                                   e.tensor_copy(comb[:, rnd * 512:(rnd + 1) * 512],
                                                 PB[6][:, :]),
                                   deps=[tk, dbg_comb], mark=True)
                        ctev.append(cte)
                        combT_ev = cte
                    COMBT_T = tk
                    for nh2 in range(2):
                        for kc in range(8):
                            p.op("t", lambda e, nh2=nh2, kc=kc:
                                 e.matmul(PB[6 + nh2][:, :],
                                          comb[:, kc * 128:(kc + 1) * 128],
                                          wo_t[:, kc * 1024 + nh2 * 512: kc * 1024 + (nh2 + 1) * 512],
                                          start=(kc == 0), stop=(kc == 7)),
                                 deps=[ctev[0], ctev[1], WO, lat_ev[1]])
                    WOMM = p.mark("t")
                    comb_rd = WOMM

                    if KPART < 4:
                        continue
                    XRE = p.dma("y", x1_tc[:, :], x_comb[2 + tc, :, :], "xresem",
                                deps=x1war)
                    ADD0 = p.op("v", lambda e:
                                e.scalar_tensor_tensor(x1_tc[:, 0:512], x1_tc[:, 0:512],
                                                       1.0, PB[6][:, :], ALU.mult, ALU.add,
                                                       accum_out=stat[:, 32:33]),
                                deps=[WOMM, XRE], mark=True)
                    ADD1 = p.op("v", lambda e:
                                e.scalar_tensor_tensor(x1_tc[:, 512:1024], x1_tc[:, 512:1024],
                                                       1.0, PB[7][:, :], ALU.mult, ALU.add,
                                                       accum_out=stat[:, 33:34]),
                                deps=[WOMM, XRE], mark=True)
                    wo_rd[:] = [ADD0, ADD1]
                    combT_ev = ADD1
                    if tc + 1 < min(NT, KTC):
                        emit_roll(tc + 1)
                        emit_lsc(tc + 1)
                    SQ = p.op("s", lambda e:
                              e.activation(x1Tf[:, :], x1_tc[:, :], AF.Square,
                                           accum_out=stat[:, 4:5]),
                              deps=[ADD0, ADD1, sq_prev, x1Tf_rd, COMBT_T], mark=True)
                    sq_prev = SQ
                    NM = p.op("v", lambda e: e.scalar_tensor_tensor(
                                  stat[:, 3:4], stat[:, 32:33], -1.0 / 1024,
                                  stat[:, 33:34].to_broadcast((128, 1)) if False else stat[:, 33:34],
                                  ALU.bypass, ALU.bypass),
                              deps=[ADD0, ADD1], mark=True) if False else None
                    NM1 = p.op("v", lambda e: e.tensor_tensor(stat[:, 34:35], stat[:, 32:33],
                                                              stat[:, 33:34], ALU.add),
                               deps=[ADD0, ADD1], mark=True)
                    NM2 = p.op("v", lambda e: e.tensor_scalar(stat[:, 3:4], stat[:, 34:35],
                                                              -1.0 / 1024, None, ALU.mult),
                               deps=[NM1], mark=True)
                    M2C = p.op("v", lambda e: e.tensor_tensor(stat[:, 35:36], stat[:, 3:4],
                                                              stat[:, 3:4], ALU.mult),
                               deps=[NM2], mark=True)
                    VARC = p.op("v", lambda e: e.tensor_scalar(stat[:, 36:37], stat[:, 4:5],
                                                               1.0 / 1024, None, ALU.mult),
                               deps=[SQ], mark=True)
                    VARF = p.op("v", lambda e: e.tensor_tensor(stat[:, 37:38], stat[:, 36:37],
                                                               stat[:, 35:36], ALU.subtract),
                               deps=[VARC, M2C], mark=True)
                    LNV = p.op("s", lambda e: e.activation(stat[:, 5:6], stat[:, 37:38], AF.Ln,
                                                           bias=eps_c[:, :]),
                               deps=[VARF, EPSC], mark=True)
                    RSD = p.op("s", lambda e: e.activation(stat[:, 6:7], stat[:, 5:6], AF.Exp,
                                                           scale=-0.5),
                               deps=[LNV], mark=True)
                    NMR = p.op("v", lambda e: e.tensor_tensor(stat[:, 7:8], stat[:, 3:4],
                                                              stat[:, 6:7], ALU.mult),
                               deps=[RSD, NM2], mark=True)
                    X1 = p.op("v", lambda e:
                              e.tensor_scalar(x1_tc[:, :], x1_tc[:, :], stat[:, 6:7],
                                              stat[:, 7:8], ALU.mult, ALU.add),
                              deps=[NMR, SQ], mark=True)

                    X1T = []
                    X1TF = []
                    x1t_t = None
                    for rnd in range(2):
                        tb = 2 + rnd
                        for q4 in range(4):
                            dc = rnd * 4 + q4
                            p.op("t", lambda e, dc=dc, q4=q4, tb=tb:
                                 e.transpose(PB[tb][:, q4 * 128:(q4 + 1) * 128],
                                             x1_tc[:, dc * 128:(dc + 1) * 128], id_f[:, :]),
                                 deps=[X1, kv_ev[2], kv_ev[3], rem_rd[0], rem_rd[1]])
                        x1t_t = p.mark("t")
                        EB = p.op("v", lambda e, rnd=rnd, tb=tb:
                                  e.tensor_copy(x1T_tc[:, rnd * 512:(rnd + 1) * 512], PB[tb][:, :]),
                                  deps=[x1t_t, x1T_rd], mark=True)
                        EF = p.op("v", lambda e, rnd=rnd, tb=tb:
                                  e.tensor_copy(x1Tf[:, rnd * 512:(rnd + 1) * 512], PB[tb][:, :]),
                                  deps=[x1t_t, x1Tf_rd, sq_prev], mark=True)
                        X1T.append(EB)
                        X1TF.append(EF)
                    pb23_rd[:] = [X1TF[0], X1TF[1]]

                    if KPART < 5:
                        continue
                    rslot = tc % 2
                    roff = 256 + rslot * 16
                    for kc in range(8):
                        p.op("t", lambda e, kc=kc, roff=roff:
                             e.matmul(PB[5][:, roff:roff + 12],
                                      x1Tf[:, kc * 128:(kc + 1) * 128],
                                      lrw[:, kc * 12:(kc + 1) * 12],
                                      start=(kc == 0), stop=False),
                             deps=X1TF + [CONST, rtr_rd[rslot], pb5_ex])
                    p.op("t", lambda e, roff=roff:
                         e.matmul(PB[5][:, roff:roff + 12], ones_f[:, :], lrb[:, :],
                                  start=False, stop=True),
                         deps=[ONESF, CONST])
                    RMM = p.mark("t")
                    x1Tf_rd = RMM
                    E4 = p.op("s", lambda e, roff=roff:
                              e.activation(lp_pad[:, 0:4], PB[5][:, roff:roff + 4], AF.Exp,
                                           accum_out=stat[:, 24:25]),
                              deps=[RMM, LPP0], mark=True)
                    R4 = p.op("v", lambda e: e.reciprocal(stat[:, 25:26], stat[:, 24:25]),
                              deps=[E4], mark=True)
                    LP4 = p.op("v", lambda e: e.tensor_scalar(lp_pad[:, 0:4], lp_pad[:, 0:4],
                                                              stat[:, 25:26], None, ALU.mult),
                               deps=[R4], mark=True)
                    EW0 = p.op("v", lambda e: e.tensor_tensor(rl_sb[:, 0:4], lp_pad[:, 0:4],
                                                              lp_pad[:, 1:5], ALU.add),
                               deps=[LP4, LPP0], mark=True)
                    EW1 = p.op("v", lambda e: e.tensor_tensor(rl_sb[:, 4:8], lp_pad[:, 0:4],
                                                              lp_pad[:, 1:5], ALU.add),
                               deps=[LP4, LPP0], mark=True)
                    RL = p.op("v", lambda e, roff=roff:
                              e.scalar_tensor_tensor(rl_sb[:, :], rl_sb[:, :], 0.1,
                                                     PB[5][:, roff + 4:roff + 12], ALU.mult, ALU.add),
                              deps=[EW0, EW1, RMM], mark=True)
                    E8 = p.op("s", lambda e: e.activation(p8_sb[:, :], rl_sb[:, :], AF.Exp,
                                                          accum_out=stat[:, 26:27]),
                              deps=[RL], mark=True)
                    R8 = p.op("v", lambda e: e.reciprocal(stat[:, 27:28], stat[:, 26:27]),
                              deps=[E8], mark=True)
                    PP = p.op("v", lambda e: e.tensor_scalar(p8_sb[:, :], p8_sb[:, :],
                                                             stat[:, 27:28], None, ALU.mult),
                              deps=[R8], mark=True)
                    M1 = p.op("v", lambda e: e.reduce_max(stat[:, 28:29], p8_sb[:, :], axis=AX),
                              deps=[PP], mark=True)
                    MRa = p.op("v", lambda e: e.tensor_scalar(pm_sb[:, :], p8_sb[:, :],
                                                              stat[:, 28:29], None, ALU.is_ge),
                               deps=[M1, PP], mark=True)
                    MR = p.op("v", lambda e: e.scalar_tensor_tensor(pm_sb[:, :], pm_sb[:, :],
                                                                    -2.0, p8_sb[:, :],
                                                                    ALU.mult, ALU.add),
                              deps=[MRa], mark=True)
                    M2 = p.op("v", lambda e: e.reduce_max(stat[:, 29:30], pm_sb[:, :], axis=AX),
                              deps=[MR], mark=True)
                    S12 = p.op("v", lambda e: e.tensor_tensor(stat[:, 30:31], stat[:, 28:29],
                                                              stat[:, 29:30], ALU.add),
                               deps=[M2, M1], mark=True)
                    R12 = p.op("v", lambda e: e.reciprocal(stat[:, 31:32], stat[:, 30:31]),
                               deps=[S12], mark=True)
                    GEm = p.op("v", lambda e: e.tensor_scalar(ge_sb[:, :], p8_sb[:, :],
                                                              stat[:, 29:30], None, ALU.is_ge),
                               deps=[M2, PP], mark=True)
                    CC = p.op("v", lambda e, tc=tc:
                              e.scalar_tensor_tensor(c_sb[:, tc * 8:(tc + 1) * 8], p8_sb[:, :],
                                                     stat[:, 31:32], ge_sb[:, :], ALU.mult, ALU.mult),
                              deps=[R12, GEm], mark=True)
                    cc_toks.append(CC)
                    rtr_rd[rslot] = CC

                    SX1 = p.dma("y", x1s[tc, :, :], x1_tc[:, :], "x1sem", deps=[X1])
                    SXT = p.dma("y", x1Ts[:, :, tc * 128:(tc + 1) * 128].rearrange("k p m -> p k m"),
                                x1T_tc[:, :].rearrange("p (k m) -> p k m", k=8),
                                f"x1tsem{tc % 2}", deps=X1T)
                    x1war = [SX1, x1t_t, sq_prev, X1]
                    x1T_rd = SXT

                    if tc == 0:
                        dtk = dump("comb0", comb[:, :], [128, 1024], BF16, [COMBDONE] + rg)
                        if dtk:
                            dbg_comb = dtk
                        dtk = dump("x1_0", x1_tc[:, :], [128, 1024], F32, [X1])
                        if dtk:
                            x1war = x1war + [dtk]
                        dump("c0", c_sb[:, 0:8], [128, 8], F32, [CC])

                X1SALL = ("x1sem", p.cnt["x1sem"])
                X1TALL = ("x1tsem0", p.cnt["x1tsem0"])
                X1TALL1 = ("x1tsem1", p.cnt["x1tsem1"])
                es_e.close()
                FT = ("pt", p.cnt["pt"])
                FV = ("pv", p.cnt["pv"])
                FS = ("ps", p.cnt["ps"])
                FGATE = [FT, FV, FS, X1SALL, X1TALL, X1TALL1]
                if DBG:
                    FGATE.extend(dbg[k] for k in dbg)
                EPB = {
                    0: st_ev[0], 1: st_ev[1], 2: pb23_rd[0], 3: pb23_rd[1],
                    4: ("pv", p.cnt["pv"]), 5: rtr_rd[1],
                    6: wo_rd[0], 7: wo_rd[1],
                }
                CCLAST = CC
                EPB5X = pb5_ex

            if KSTAGE < 3:
                raise _SkipRest()
            with ExitStack() as es_ffn:
                def sbf(name, shape, dt):
                    return es_ffn.enter_context(nc.sbuf_tensor(name, shape, dt))

                acc = sbf("acc", [128, NT * 1024], F32)
                x1T_sb = sbf("x1T_sb", [128, 8 * 2048], BF16)
                hT = sbf("hT", [128, 32 * 1024], BF16)
                es_f1 = ExitStack()
                def sbf1(name, shape, dt):
                    return es_f1.enter_context(nc.sbuf_tensor(name, shape, dt))
                w1b = [sbf1(f"w1b{i}", [128, 1024], BF16) for i in range(4)]
                w2b = [sbf1(f"w2b{i}", [128, 8192], BF16) for i in range(2)]
                b1all = sbf1("b1all", [128, 256], F32)

                p.new_sem("xtsem")
                XTL = p.dma("y", x1T_sb[:, :].rearrange("p (k m) -> p k m", k=8),
                            x1Ts.rearrange("k p m -> p k m"), "xtsem", deps=FGATE)
                p.new_sem("accsem")
                ACCL = p.dma("y", acc[:, :].rearrange("p (t c) -> p t c", t=NT),
                             x1s.rearrange("t p c -> p t c"), "accsem", deps=FGATE)
                p.new_sem("b1sem")
                B1L = p.dma("y", b1all[:, :].rearrange("p (e f) -> p e f", e=8),
                            b1_in.rearrange("e p f -> p e f"), "b1sem", deps=FGATE)
                for i in range(4):
                    p.new_sem(f"w1sem{i}")
                for i in range(2):
                    p.new_sem(f"w2sem{i}")

                w1_war = [None, None, None, None]
                w2_war = [None, None]
                w1_tok = {}
                w2_tok = {}

                seq_w1 = [(e_, Tt, f) for e_ in range(NE) for Tt in range(2) for f in range(32)]
                seq_w2 = [(e_, Tt, dq) for e_ in range(NE) for Tt in range(2) for dq in range(4)]

                def load_w1(idx):
                    e_, Tt, f = seq_w1[idx]
                    sl = idx % 4
                    tok = p.dma("g", w1b[sl][:, :], w1_in[e_, f, :, :], f"w1sem{sl}",
                                deps=[w1_war[sl]] + (FGATE if idx < 4 else []))
                    w1_tok[idx] = (tok, sl)

                def load_w2(idx):
                    e_, Tt, dq = seq_w2[idx]
                    sl = idx % 2
                    tok = None
                    for c4 in range(4):
                        tok = p.dma("g", w2b[sl][:, c4 * 2048:(c4 + 1) * 2048],
                                    w2_in[e_, dq, :, c4 * 2048:(c4 + 1) * 2048], f"w2sem{sl}",
                                    deps=([w2_war[sl]] if c4 == 0 else []) +
                                         (FGATE if idx < 2 and c4 == 0 else []))
                    w2_tok[idx] = (tok, sl)

                load_w1(0)
                load_w1(1)
                load_w1(2)
                load_w1(3)
                load_w2(0)
                load_w2(1)

                ct_ev = None
                for rnd in range(4):
                    cb = 6 + rnd % 2
                    for j in range(4):
                        tcg = rnd * 4 + j
                        p.op("t", lambda e, tcg=tcg, j=j, cb=cb:
                             e.transpose(PB[cb][0:8, j * 128:(j + 1) * 128],
                                         c_sb[:, tcg * 8:(tcg + 1) * 8], id_f[:, :]),
                             deps=[cc_toks[tcg], EPB[6] if cb == 6 else EPB[7], ct_ev])
                    tkc = p.mark("t")
                    ct_ev = p.op("v", lambda e, rnd=rnd, cb=cb:
                                 e.tensor_copy(cT[0:8, rnd * 512:(rnd + 1) * 512],
                                               PB[cb][0:8, 0:512]),
                                 deps=[tkc], mark=True)
                CT_LAST = ct_ev

                o_ev = [EPB[4], EPB[5]]
                h_rd = [EPB[0], EPB[1], EPB[2], EPB[3]]
                MM2LAST = None
                iw1 = 0
                iw2 = 0

                for e_ in range(NE):
                    for Tt in range(2):
                        HDONE = None
                        for f in range(32):
                            tokw, sl = w1_tok[iw1]
                            tkmm = None
                            for nh in range(2):
                                bk = 2 * (f % 2) + nh
                                for kc in range(8):
                                    p.op("t", lambda e, f=f, nh=nh, kc=kc, sl=sl, Tt=Tt, bk=bk:
                                         e.matmul(PB[bk][:, :],
                                                  w1b[sl][:, kc * 128:(kc + 1) * 128],
                                                  x1T_sb[:, kc * 2048 + Tt * 1024 + nh * 512:
                                                         kc * 2048 + Tt * 1024 + (nh + 1) * 512],
                                                  start=(kc == 0), stop=(kc == 7)),
                                         deps=[tokw, XTL, h_rd[bk], MM2LAST])
                                tkmm = p.mark("t")
                                GL = p.op("s", lambda e, f=f, nh=nh, bk=bk:
                                          e.activation(hT[:, f * 1024 + nh * 512:
                                                          f * 1024 + (nh + 1) * 512],
                                                       PB[bk][:, :], AF.Gelu,
                                                       bias=b1all[:, e_ * 32 + f: e_ * 32 + f + 1]),
                                          deps=[tkmm, B1L, MM2LAST], mark=True)
                                h_rd[bk] = GL
                                HDONE = GL
                            w1_war[sl] = tkmm
                            if iw1 + 4 < len(seq_w1):
                                load_w1(iw1 + 4)
                            iw1 += 1
                        for dq in range(4):
                            tokw2, sl2 = w2_tok[iw2]
                            for tcl in range(8):
                                ob = 4 + (tcl % 2)
                                gtc = Tt * 8 + tcl
                                for f in range(32):
                                    p.op("t", lambda e, f=f, tcl=tcl, sl2=sl2, ob=ob:
                                         e.matmul(PB[ob][:, 0:256],
                                                  hT[:, f * 1024 + tcl * 128:
                                                     f * 1024 + tcl * 128 + 128],
                                                  w2b[sl2][:, f * 256:(f + 1) * 256],
                                                  start=(f == 0), stop=(f == 31)),
                                         deps=[tokw2, HDONE, o_ev[tcl % 2],
                                               EPB5X if ob == 5 else None])
                                tk2 = p.mark("t")
                                o_ev[tcl % 2] = p.op(
                                    "v", lambda e, gtc=gtc, dq=dq, e_=e_, ob=ob:
                                    e.scalar_tensor_tensor(
                                        acc[:, gtc * 1024 + dq * 256: gtc * 1024 + (dq + 1) * 256],
                                        PB[ob][:, 0:256],
                                        c_sb[:, gtc * 8 + e_: gtc * 8 + e_ + 1],
                                        acc[:, gtc * 1024 + dq * 256: gtc * 1024 + (dq + 1) * 256],
                                        ALU.mult, ALU.add),
                                    deps=[tk2, ACCL, CCLAST], mark=True)
                            w2_war[sl2] = tk2
                            if iw2 + 2 < len(seq_w2):
                                load_w2(iw2 + 2)
                            iw2 += 1
                        MM2LAST = tk2

                ACCDONE = max(o_ev[0], o_ev[1], key=lambda t: t[1])
                es_f1.close()
                b2_sb = sbf("b2_sb", [8, 1024], BF16)
                p.new_sem("b2sem")
                B2L = p.dma("g", b2_sb[:, :], b2_in, "b2sem",
                            deps=[MM2LAST, ACCDONE, h_rd[0], h_rd[1], h_rd[2], h_rd[3]])

                p.new_sem("osem0")
                p.new_sem("osem1")
                b2done = None
                y2_war = [None]
                for gtc in range(NT):
                    for nh2 in range(2):
                        p.op("t", lambda e, gtc=gtc, nh2=nh2:
                             e.matmul(PB[6 + nh2][:, :],
                                      cT[0:8, gtc * 128:(gtc + 1) * 128],
                                      b2_sb[0:8, nh2 * 512:(nh2 + 1) * 512],
                                      start=True, stop=True),
                             deps=[B2L, CT_LAST, EPB[6], EPB[7], b2done])
                    BTK = p.mark("t")
                    A0 = p.op("v", lambda e, gtc=gtc:
                              e.tensor_tensor(acc[:, gtc * 1024: gtc * 1024 + 512],
                                              acc[:, gtc * 1024: gtc * 1024 + 512],
                                              PB[6][:, :], ALU.add),
                              deps=[BTK, o_ev[0], o_ev[1], ACCDONE], mark=True)
                    A1 = p.op("v", lambda e, gtc=gtc:
                              e.tensor_tensor(acc[:, gtc * 1024 + 512: gtc * 1024 + 1024],
                                              acc[:, gtc * 1024 + 512: gtc * 1024 + 1024],
                                              PB[7][:, :], ALU.add),
                              deps=[BTK], mark=True)
                    b2done = A1
                    NS = p.op("v", lambda e, gtc=gtc:
                              e.reduce_sum(stat[:, 2:3], acc[:, gtc * 1024:(gtc + 1) * 1024],
                                           axis=AX, negate=True),
                              deps=[A0, A1], mark=True)
                    NM = p.op("v", lambda e: e.tensor_scalar(stat[:, 3:4], stat[:, 2:3],
                                                             1.0 / 1024, None, ALU.mult),
                              deps=[NS], mark=True)
                    SQ = p.op("s", lambda e, gtc=gtc:
                              e.activation(hT[:, 0:1024], acc[:, gtc * 1024:(gtc + 1) * 1024],
                                           AF.Square, bias=stat[:, 3:4], accum_out=stat[:, 4:5]),
                              deps=[A0, A1, NM, y2_war[0], MM2LAST], mark=True)
                    y2_war[0] = SQ
                    LNV = p.op("s", lambda e: e.activation(stat[:, 5:6], stat[:, 4:5], AF.Ln,
                                                           scale=1.0 / 1024, bias=eps_c[:, :]),
                               deps=[SQ, EPSC], mark=True)
                    RSD = p.op("s", lambda e: e.activation(stat[:, 6:7], stat[:, 5:6], AF.Exp,
                                                           scale=-0.5),
                               deps=[LNV], mark=True)
                    NMR = p.op("v", lambda e: e.tensor_tensor(stat[:, 7:8], stat[:, 3:4],
                                                              stat[:, 6:7], ALU.mult),
                               deps=[RSD, NM], mark=True)
                    OA = p.op("v", lambda e, gtc=gtc:
                              e.tensor_scalar(acc[:, gtc * 1024:(gtc + 1) * 1024],
                                              acc[:, gtc * 1024:(gtc + 1) * 1024],
                                              stat[:, 6:7], stat[:, 7:8], ALU.mult, ALU.add),
                              deps=[NMR, SQ], mark=True)
                    ST = p.dma("y", out_ap[gtc * 128:(gtc + 1) * 128, :],
                               acc[:, gtc * 1024:(gtc + 1) * 1024],
                               f"osem{gtc % 2}", deps=[OA])

                p.wait("y", ("osem0", p.cnt["osem0"]))
                p.wait("y", ("osem1", p.cnt["osem1"]))
                if DBG:
                    for k in dbg:
                        p.wait("y", dbg[k])

        except _SkipRest:
            pass

        with nc.Block() as block:
            p.realize(block)

    return nc, sorted(dbg.keys())


class _SkipRest(Exception):
    pass



def _build_masks():
    m = np.zeros((128, 4 * 512), dtype=BF16NP)
    for var in range(4):
        par, fb = var % 2, var // 2
        kp = np.arange(128)[:, None]
        col = np.arange(512)[None, :]
        kc, q = col // 128, col % 128
        j = kc * 128 + kp
        i = par * 128 + q
        ok = (j > i) & (j <= i + 256)
        if fb:
            ok = ok & (j >= 256)
        m[:, var * 512:(var + 1) * 512] = ok.astype(BF16NP)
    return m


def _build_latmask(half):
    m = np.zeros((128, NT * 64), dtype=BF16NP)
    for tc in range(NT):
        pcol = np.arange(128)[:, None]
        n_rel = np.arange(64)[None, :]
        s_glob = half * 2048 + tc * 128 + pcol
        n_g = np.where(n_rel < 32, half * 32 + n_rel, (1 - half) * 32 + (n_rel - 32))
        ok = n_g <= (s_glob // 64)
        m[:, tc * 64:(tc + 1) * 64] = ok.astype(BF16NP)
    return m


def kernel(L0_attn_x, L0_attn_wq, L0_attn_wk, L0_attn_wv, L0_attn_wgq, L0_attn_wgk,
           L0_attn_wgv, L0_attn_r_out_w, L0_attn_gate_local_w, L0_attn_gate_remote_w,
           L0_attn_w_out, L0_ffn_router_w, L0_ffn_router_b, L0_ffn_lang_w, L0_ffn_lang_b,
           L0_ffn_w1, L0_ffn_b1, L0_ffn_w2, L0_ffn_b2):
    f32 = lambda a: np.ascontiguousarray(np.asarray(a, dtype=np.float32))
    x = f32(L0_attn_x)
    wq = f32(L0_attn_wq); wk = f32(L0_attn_wk); wv = f32(L0_attn_wv)
    wgq = f32(L0_attn_wgq); wgk = f32(L0_attn_wgk); wgv = f32(L0_attn_wgv)
    ro = f32(L0_attn_r_out_w); glw = f32(L0_attn_gate_local_w)
    grw = f32(L0_attn_gate_remote_w); wo = f32(L0_attn_w_out)
    rw = f32(L0_ffn_router_w); rb = f32(L0_ffn_router_b)
    lw = f32(L0_ffn_lang_w); lb = f32(L0_ffn_lang_b)
    w1 = f32(L0_ffn_w1); b1 = f32(L0_ffn_b1); w2 = f32(L0_ffn_w2); b2 = f32(L0_ffn_b2)

    maskT_all = _build_masks()
    maskT_h1 = maskT_all.copy()
    maskT_h1[:, 2 * 512:4 * 512] = maskT_all[:, 0:2 * 512]
    shared = {
        "ident": np.eye(128, dtype=np.float32),
        "poolmat": np.kron(np.eye(2, dtype=np.float32),
                           np.full((64, 1), 1.0 / 64, np.float32)),

        "glw": glw, "grw": grw,
        "lrw": np.ascontiguousarray(
            np.concatenate([lw.T, rw.T], axis=1).reshape(8, 128, 12)),
        "lrb": np.concatenate([lb, rb])[None, :].astype(np.float32),
        "wq": np.ascontiguousarray(wq.reshape(8, 128, 8, 128).transpose(1, 0, 2, 3).reshape(128, 8192)),
        "wk": np.ascontiguousarray(wk.reshape(8, 128, 8, 128).transpose(1, 0, 2, 3).reshape(128, 8192)),
        "wo": np.ascontiguousarray(wo.reshape(8, 128, 8, 128).transpose(1, 0, 2, 3).reshape(128, 8192)),
        "wv": np.ascontiguousarray(wv.reshape(8, 128, 1024).transpose(1, 0, 2).reshape(128, 8192)),
        "wgq": np.ascontiguousarray(wgq.reshape(8, 128, 4, 128).transpose(1, 0, 2, 3).reshape(128, 4096)),
        "wgk": np.ascontiguousarray(wgk.reshape(8, 128, 4, 128).transpose(1, 0, 2, 3).reshape(128, 4096)),
        "wgv": np.ascontiguousarray(wgv.reshape(8, 128, 512).transpose(1, 0, 2).reshape(128, 4096)),
        "ro": np.ascontiguousarray(ro.reshape(8, 64, 1024).transpose(1, 0, 2).reshape(64, 8192)),
        "b1": np.ascontiguousarray(b1.reshape(8, 32, 128).transpose(0, 2, 1)),
        "b2": b2,
        "w1": np.ascontiguousarray(
            w1.reshape(8, 8, 128, 32, 128).transpose(0, 3, 2, 1, 4).reshape(8, 32, 128, 1024)),
        "w2": np.ascontiguousarray(
            w2.reshape(8, 32, 128, 4, 256).transpose(0, 3, 2, 1, 4).reshape(8, 4, 128, 8192)),
    }
    latmasks = [_build_latmask(0), _build_latmask(1)]

    in_maps = []
    for c in range(8):
        b, half = c // 2, c % 2
        own = x[b, half * 2048:(half + 1) * 2048]
        halo = (x[b, 2048 - 256:2048] if half == 1 else np.zeros((256, D), np.float32))
        other = x[b, (1 - half) * 2048:(2 - half) * 2048]
        x_cb = np.ascontiguousarray(
            np.concatenate([halo, own, other], axis=0).reshape(34, 128, D))
        m = dict(shared)
        m["x_comb"] = x_cb
        m["latmask"] = latmasks[half]
        m["maskT"] = maskT_all if half == 0 else maskT_h1
        in_maps.append(m)

    nc, dbg_names = build_program()
    res = run_bass_kernel_spmd(nc, in_maps, core_ids=list(range(8)))

    out = np.empty((4, 4096, D), dtype=np.float32)
    for c in range(8):
        b, half = c // 2, c % 2
        out[b, half * 2048:(half + 1) * 2048] = res.results[c]["out"]
    if DBG:
        kernel._dbg = [{k: r.get("dbg_" + k) for k in dbg_names} for r in res.results]
    return out


if __name__ == "__main__":
    nc, names = build_program()
    print("built ok; dbg:", names)


# version 11 (final)
# speedup vs baseline: 1.0855x; 1.0035x over previous
import os
from contextlib import ExitStack

import numpy as np
import ml_dtypes

import concourse.bass as bass
import concourse.mybir as mybir
from concourse.bass_utils import run_bass_kernel_spmd

F32 = mybir.dt.float32
BF16 = mybir.dt.bfloat16
AF = mybir.ActivationFunctionType
ALU = mybir.AluOpType
AX = mybir.AxisListType.X
BF16NP = ml_dtypes.bfloat16

D = 1024
NT = 16
NH = 16
NLH = 8
NE = 8
DBG = os.environ.get("KDBG", "") == "1"
KSTAGE = int(os.environ.get("KSTAGE", "3"))
KTC = int(os.environ.get("KTC", "16"))
KPART = int(os.environ.get("KPART", "9"))
KLAT = int(os.environ.get("KLAT", "9"))

ENGS = ("t", "v", "s", "g", "y")


class P:

    def __init__(self, nc, es):
        self.nc = nc
        self.es = es
        self.ops = {e: [] for e in ENGS}
        self.sem = {}
        self.cnt = {}
        self.hw = {e: {} for e in ENGS}
        for e in ENGS:
            self.new_sem("p" + e)

    def new_sem(self, name):
        self.sem[name] = self.es.enter_context(self.nc.semaphore(name))
        self.cnt[name] = 0
        return name

    def wait(self, eng, tok):
        if tok is None:
            return
        name, val = tok
        if self.hw[eng].get(name, 0) >= val:
            return
        self.hw[eng][name] = val
        h = self.sem[name]
        self.ops[eng].append(lambda e, h=h, v=val: e.wait_ge(h, v))

    def op(self, eng, fn, deps=(), mark=False):
        for t in deps:
            self.wait(eng, t)
        if mark:
            name = "p" + eng
            self.cnt[name] += 1
            tok = (name, self.cnt[name])
            h = self.sem[name]
            self.ops[eng].append(lambda e, fn=fn, h=h: fn(e).then_inc(h, 1))
            return tok
        self.ops[eng].append(fn)
        return None

    def mark(self, eng):
        name = "p" + eng
        self.cnt[name] += 1
        tok = (name, self.cnt[name])
        h = self.sem[name]
        lst = self.ops[eng]
        assert lst, "mark with no prior op"
        prev = lst.pop()
        def wrapped(e, prev=prev, h=h):
            ins = prev(e)
            ins.then_inc(h, 1)
            return ins
        lst.append(wrapped)
        return tok

    def dma(self, eng, out_ap, in_ap, sem_name, deps=(), **kw):
        for t in deps:
            self.wait(eng, t)
        self.cnt[sem_name] += 16
        tok = (sem_name, self.cnt[sem_name])
        h = self.sem[sem_name]
        self.ops[eng].append(
            lambda e, o=out_ap, i=in_ap, h=h, kw=kw: e.dma_start(o, i, **kw).then_inc(h, 16))
        return tok

    def realize(self, block):
        def body(lst):
            return lambda e: [fn(e) for fn in lst]
        block.tensor(body(self.ops["t"]))
        block.vector(body(self.ops["v"]))
        block.scalar(body(self.ops["s"]))
        block.gpsimd(body(self.ops["g"]))
        block.sync(body(self.ops["y"]))


def build_program():
    nc = bass.Bass("TRN2", target_bir_lowering=False)
    es = ExitStack()

    def din(name, shape, dt=F32):
        return nc.dram_tensor(name, shape, dt, kind="ExternalInput").ap()

    x_comb = din("x_comb", [34, 128, D])
    maskT_in = din("maskT", [128, 4 * 512], BF16)
    latmask_in = din("latmask", [128, NT * 64], BF16)
    ident_in = din("ident", [128, 128])
    poolmat_in = din("poolmat", [128, 2])
    glw_in = din("glw", [NH, 64])
    grw_in = din("grw", [NH, 64])
    lrw_in = din("lrw", [8, 128, 12])
    lrb_in = din("lrb", [1, 12])
    wq_in = din("wq", [128, 8192])
    wk_in = din("wk", [128, 8192])
    wv_in = din("wv", [128, 8192])
    wo_in = din("wo", [128, 8192])
    wgq_in = din("wgq", [128, 4096])
    wgk_in = din("wgk", [128, 4096])
    wgv_in = din("wgv", [128, 4096])
    ro_in = din("ro", [64, 8192])
    b1_in = din("b1", [NE, 128, 32])
    b2_in = din("b2", [NE, D])
    w1_in = din("w1", [NE, 32, 128, 1024])
    w2_in = din("w2", [NE, 4, 128, 8192])

    out_ap = nc.dram_tensor("out", [2048, D], F32, kind="ExternalOutput").ap()
    x1s = nc.dram_tensor("x1s", [NT, 128, D], F32).ap()
    x1Ts = nc.dram_tensor("x1Ts", [8, 128, 2048], BF16).ap()
    wk_bf = nc.dram_tensor("wk_bf", [128, 8192], BF16).ap()
    wv_bf = nc.dram_tensor("wv_bf", [128, 8192], BF16).ap()

    dbg = {}

    with es:
        p = P(nc, es)

        def sb(name, shape, dt):
            return es.enter_context(nc.sbuf_tensor(name, shape, dt))

        def dump(name, ap, shape, dt, deps):
            if not DBG:
                return None
            t = nc.dram_tensor("dbg_" + name, shape, dt, kind="ExternalOutput").ap()
            sm = p.new_sem("dbg_" + name)
            dbg[name] = p.dma("y", t, ap, sm, deps=deps)
            return dbg[name]

        id_f = sb("id_f", [128, 128], F32)
        ones_f = sb("ones_f", [1, 128], F32)
        poolmat = sb("poolmat_s", [128, 2], F32)
        Wg = sb("Wg", [128, 8 * 32], BF16)
        c_sb = sb("c_sb", [128, NT * 8], F32)
        cT = sb("cT", [8, NT * 128], BF16)
        lrw = sb("lrw_s", [128, 8 * 12], F32)
        lrb = sb("lrb_s", [1, 12], F32)
        stat = sb("stat", [128, 48], F32)
        lp_pad = sb("lp_pad", [128, 5], F32)
        rl_sb = sb("rl_sb", [128, 8], F32)
        p8_sb = sb("p8_sb", [128, 8], F32)
        pm_sb = sb("pm_sb", [128, 8], F32)
        ge_sb = sb("ge_sb", [128, 8], F32)

        PB = [es.enter_context(nc.psum_tensor(f"pb{i}", [128, 512], F32)) for i in range(8)]

        p.new_sem("csem")
        p.dma("y", id_f[:, :], ident_in, "csem")
        p.dma("y", poolmat[:, :], poolmat_in, "csem")
        p.dma("y", lrw[:, :].rearrange("p (k c) -> p k c", k=8),
              lrw_in.rearrange("k p c -> p k c"), "csem")
        p.dma("y", lrb[:, :], lrb_in, "csem")
        CONST = ("csem", p.cnt["csem"])
        IDB = None

        eps_c = sb("eps_c", [128, 1], F32)
        EPSC = p.op("v", lambda e: e.memset(eps_c[:, :], 1e-5), mark=True)
        t_wg0 = p.op("v", lambda e: e.memset(Wg[:, :], 0.0), mark=True)
        ONESF = p.op("v", lambda e: e.memset(ones_f[:, :], 1.0), mark=True)
        LPP0 = p.op("v", lambda e: e.memset(lp_pad[:, 4:5], 0.0), mark=True)
        p.new_sem("wgsem")
        for h in range(NH):
            dc, po = h // 2, 64 * (h % 2)
            for gi, src in ((0, glw_in), (1, grw_in)):
                p.dma("g", Wg[po:po + 64, dc * 32 + 16 * gi + h: dc * 32 + 16 * gi + h + 1],
                      src[h:h + 1, :].rearrange("a b -> b a"), "wgsem",
                      deps=(t_wg0,), allow_slow_non_contiguous=True)
        WGTOK = ("wgsem", p.cnt["wgsem"])

        try:
            with ExitStack() as es_att:
                def sba(name, shape, dt):
                    return es_att.enter_context(nc.sbuf_tensor(name, shape, dt))

                xT = sba("xT", [128, 8 * 2304], BF16)
                QT = sba("QT", [128, 8 * 2048], BF16)
                GQT = sba("GQT", [128, 4 * 2048], BF16)
                GKT = sba("GKT", [128, 8 * 64], BF16)
                GV2z = sba("GV2z", [128, 512], BF16)
                glgr = sba("glgr", [128, NT * 32], F32)

                es_b = ExitStack()
                def sbb(name, shape, dt):
                    return es_b.enter_context(nc.sbuf_tensor(name, shape, dt))
                xpT = sbb("xpT", [128, 8 * 64], BF16)
                x_in = [sbb(f"x_in{i}", [128, D], F32) for i in range(4)]
                wq_t = sbb("wq_t", [128, 8192], BF16)
                wgq_t = sbb("wgq_t", [128, 4096], BF16)
                wgk_t = sbb("wgk_t", [128, 4096], BF16)
                wgv_t = sbb("wgv_t", [128, 4096], BF16)
                GV2 = sbb("GV2", [64, 512], BF16)

                def wload(name, dst, src, deps=()):
                    p.new_sem("ws_" + name)
                    return p.dma("g", dst, src, "ws_" + name, deps=deps)

                WGK = wload("wgk", wgk_t[:, :], wgk_in)
                WGV = wload("wgv", wgv_t[:, :], wgv_in)
                WQ = wload("wq", wq_t[:, :], wq_in)
                WGQ = wload("wgq", wgq_t[:, :], wgq_in)
                p.new_sem("wkcast")
                p.new_sem("wvcast")
                CKW = p.dma("g", wk_bf, wk_in, "wkcast")
                CVW = p.dma("g", wv_bf, wv_in, "wvcast")

                for i in range(4):
                    p.new_sem(f"xsem{i}")
                x_slot_rd = [None] * 4
                tev = [None, None]
                for i in range(34):
                    sl = i % 4
                    deps = [x_slot_rd[sl]] if x_slot_rd[sl] else []
                    XL = p.dma("y", x_in[sl][:, :], x_comb[i, :, :], f"xsem{sl}", deps=deps)
                    if i < 18:
                        for j in range(2):
                            bk = j % 2
                            for dc in range(4 * j, 4 * j + 4):
                                p.op("t", lambda e, b=PB[bk], dc=dc, sl=sl:
                                     e.transpose(b[:, (dc % 4) * 128:(dc % 4) * 128 + 128],
                                                 x_in[sl][:, dc * 128:(dc + 1) * 128], id_f[:, :]),
                                     deps=[XL, CONST, tev[bk]])
                            tk = p.mark("t")
                            dst = (xT[:, :].rearrange("p (dc c) -> p dc c", c=2304)
                                   [:, 4 * j:4 * j + 4, i * 128:(i + 1) * 128])
                            src = PB[bk][:, :].rearrange("p (a b) -> p a b", b=128)
                            tev[bk] = p.op("v", lambda e, d=dst, s=src: e.tensor_copy(d, s),
                                           deps=[tk], mark=True)
                    if i >= 2:
                        j2 = i - 2
                        for dc in range(8):
                            p.op("t", lambda e, sl=sl, dc=dc, j2=j2:
                                 e.matmul(PB[7][:, dc * 64 + 2 * j2: dc * 64 + 2 * j2 + 2],
                                          x_in[sl][:, dc * 128:(dc + 1) * 128],
                                          poolmat[:, :], start=True, stop=True),
                                 deps=[XL, CONST])
                        x_slot_rd[sl] = p.mark("t")
                    else:
                        x_slot_rd[sl] = tk
                POOLT = x_slot_rd[33 % 4]
                XTD = [tev[0], tev[1]]
                XP = p.op("v", lambda e: e.tensor_copy(xpT[:, :], PB[7][:, :]),
                          deps=[POOLT], mark=True)
                dump("xpT", xpT[:, :], [128, 512], BF16, [XP])

                if KSTAGE < 1:
                    es_b.close()
                    es_att.close()
                    raise _SkipRest()
                for gc in range(4):
                    for kc in range(8):
                        p.op("t", lambda e, gc=gc, kc=kc:
                             e.matmul(PB[5][:, gc * 64:(gc + 1) * 64],
                                      wgk_t[:, kc * 512 + gc * 128: kc * 512 + gc * 128 + 128],
                                      xpT[:, kc * 64:(kc + 1) * 64],
                                      start=(kc == 0), stop=(kc == 7)),
                             deps=[XP, WGK])
                GKMM = p.mark("t")
                GKZ0 = p.op("v", lambda e: e.memset(GKT[:, :], 0.0), mark=True)
                for lh in range(8):
                    GKEV = p.op("v", lambda e, lh=lh:
                                e.tensor_copy(GKT[64 * (lh % 2):64 * (lh % 2) + 64,
                                                  lh * 64:(lh + 1) * 64],
                                              PB[5][64 * (lh % 2):64 * (lh % 2) + 64,
                                                    (lh // 2) * 64:(lh // 2) * 64 + 64]),
                                deps=[GKMM, GKZ0], mark=True)
                for kc in range(8):
                    p.op("t", lambda e, kc=kc:
                         e.matmul(PB[6][0:64, :], xpT[:, kc * 64:(kc + 1) * 64],
                                  wgv_t[:, kc * 512:(kc + 1) * 512],
                                  start=(kc == 0), stop=(kc == 7)),
                         deps=[XP, WGV])
                GVMM = p.mark("t")
                GVZ0 = p.op("v", lambda e: e.memset(GV2z[:, :], 0.0), mark=True)
                GVEV = p.op("v", lambda e: e.tensor_copy(GV2[0:64, :], PB[6][0:64, :]),
                            deps=[GVMM], mark=True)
                p.new_sem("gv2sem")
                ev_dst = GV2z[0:64, :].rearrange("p (l c) -> p l c", c=128)[:, :, 0:64]
                ev_src = GV2[0:64, :].rearrange("p (l c) -> p l c", c=128)[:, :, 0:64]
                od_dst = GV2z[64:128, :].rearrange("p (l c) -> p l c", c=128)[:, :, 64:128]
                od_src = GV2[0:64, :].rearrange("p (l c) -> p l c", c=128)[:, :, 64:128]
                p.dma("y", ev_dst, ev_src, "gv2sem", deps=[GVEV, GVZ0])
                GV2D = p.dma("y", od_dst, od_src, "gv2sem", deps=[GVEV, GVZ0])
                dump("GKT", GKT[:, :], [128, 512], BF16, [GKEV])
                dump("GV2", GV2z[:, :], [128, 512], BF16, [GV2D])

                qev = [None, None]
                for dc in range(8):
                    for n4 in range(4):
                        bk = (dc * 4 + n4) % 2
                        for kc in range(8):
                            p.op("t", lambda e, dc=dc, n4=n4, kc=kc, b=PB[2 + bk]:
                                 e.matmul(b[:, :],
                                          wq_t[:, kc * 1024 + dc * 128: kc * 1024 + dc * 128 + 128],
                                          xT[:, kc * 2304 + 256 + n4 * 512: kc * 2304 + 256 + (n4 + 1) * 512],
                                          start=(kc == 0), stop=(kc == 7)),
                                 deps=[XTD[0], XTD[1], WQ, qev[bk]])
                        tk = p.mark("t")
                        qev[bk] = p.op("v", lambda e, dc=dc, n4=n4, b=PB[2 + bk]:
                                       e.tensor_copy(QT[:, dc * 2048 + n4 * 512: dc * 2048 + (n4 + 1) * 512],
                                                     b[:, :]),
                                       deps=[tk], mark=True)
                QDONE = [qev[0], qev[1]]
                for dc in range(4):
                    for n4 in range(4):
                        bk = (dc * 4 + n4) % 2
                        for kc in range(8):
                            p.op("t", lambda e, dc=dc, n4=n4, kc=kc, b=PB[2 + bk]:
                                 e.matmul(b[:, :],
                                          wgq_t[:, kc * 512 + dc * 128: kc * 512 + dc * 128 + 128],
                                          xT[:, kc * 2304 + 256 + n4 * 512: kc * 2304 + 256 + (n4 + 1) * 512],
                                          start=(kc == 0), stop=(kc == 7)),
                                 deps=[XTD[0], XTD[1], WGQ, qev[bk]])
                        tk = p.mark("t")
                        qev[bk] = p.op("v", lambda e, dc=dc, n4=n4, b=PB[2 + bk]:
                                       e.tensor_copy(GQT[:, dc * 2048 + n4 * 512: dc * 2048 + (n4 + 1) * 512],
                                                     b[:, :]),
                                       deps=[tk], mark=True)
                GQDONE = [qev[0], qev[1]]
                dump("QT", QT[:, 0:2048], [128, 2048], BF16, QDONE)
                for tc in range(NT):
                    for kc in range(8):
                        p.op("t", lambda e, tc=tc, kc=kc:
                             e.matmul(PB[4][:, tc * 32:(tc + 1) * 32],
                                      QT[:, kc * 2048 + tc * 128: kc * 2048 + tc * 128 + 128],
                                      Wg[:, kc * 32:(kc + 1) * 32],
                                      start=(kc == 0), stop=(kc == 7)),
                             deps=[QDONE[0], QDONE[1], WGTOK])
                GMM = p.mark("t")
                GSIG = p.op("s", lambda e: e.activation(glgr[:, :], PB[4][:, :], AF.Sigmoid),
                            deps=[GMM], mark=True)
                dump("glgr", glgr[:, :], [128, 512], F32, [GSIG])

                if KSTAGE < 2:
                    es_b.close()
                    es_att.close()
                    raise _SkipRest()
                es_b.close()
                es_e = ExitStack()
                def sbe(name, shape, dt):
                    return es_e.enter_context(nc.sbuf_tensor(name, shape, dt))
                wk_t = sbe("wk_t", [128, 8192], BF16)
                wv_t = sbe("wv_t", [128, 8192], BF16)
                wo_t = sbe("wo_t", [128, 8192], BF16)
                ro8 = sbe("ro8", [64, 8192], BF16)
                Kroll = sbe("Kroll", [128, 16 * 512], BF16)
                Vroll = [sbe(f"Vroll{i}", [128, 16 * 65], BF16) for i in range(4)]
                Pt = [sbe(f"Pt{i}", [128, 512], BF16) for i in range(3)]
                lp_tok = sbe("lp_tok", [128, 512], BF16)
                latT8 = sbe("latT8", [64, 1024], BF16)
                comb = sbe("comb", [128, D], BF16)
                x1_tc = sbe("x1_tc", [128, D], F32)
                x1T_tc = sbe("x1T_tc", [128, D], BF16)
                x1Tf = sbe("x1Tf", [128, D], F32)
                maskT = sbe("maskT_s", [128, 4 * 512], BF16)
                latmask = sbe("latmask_s", [128, NT * 64], BF16)
                lpn = lp_tok
                lptn = lp_tok

                TCD = tk
                DBGT = max((dbg[k] for k in dbg), key=lambda t: 0) if False else None
                if DBG:
                    DBGT = None
                    dbg_gates = [dbg[k] for k in dbg]
                else:
                    dbg_gates = []
                GVDMA = ("gv2sem", p.cnt["gv2sem"])
                p.new_sem("ws_wk")
                p.new_sem("ws_wkb")
                p.new_sem("ws_wv")
                p.new_sem("ws_wvb")
                WKa = p.dma("y", wk_t[:, 0:4096], wk_bf[:, 0:4096], "ws_wk",
                            deps=[TCD, GVDMA, CKW] + dbg_gates)
                WKb = p.dma("s", wk_t[:, 4096:8192], wk_bf[:, 4096:8192], "ws_wkb",
                            deps=[TCD, GVDMA, CKW] + dbg_gates)
                WKL = [WKa, WKb]
                WVa = p.dma("y", wv_t[:, 0:4096], wv_bf[:, 0:4096], "ws_wv",
                            deps=[TCD, GVDMA, CVW] + dbg_gates)
                WVb = p.dma("s", wv_t[:, 4096:8192], wv_bf[:, 4096:8192], "ws_wvb",
                            deps=[TCD, GVDMA, CVW] + dbg_gates)
                WVL = [WVa, WVb]
                WO = wload("wo", wo_t[:, :], wo_in, deps=[TCD, GVDMA] + dbg_gates)
                RO = wload("ro", ro8[:, :], ro_in, deps=[TCD, GVDMA] + dbg_gates)
                p.new_sem("msksem")
                MASKS = p.dma("y", maskT[:, :], maskT_in, "msksem", deps=[TCD, GVDMA] + dbg_gates)
                MASKS = p.dma("y", latmask[:, :], latmask_in, "msksem", deps=[TCD, GVDMA] + dbg_gates)
                MASKS = ("msksem", p.cnt["msksem"])
                VR1 = [p.op("v", lambda e, t=t: e.memset(t[:, :], 1.0),
                            deps=[TCD, GVDMA] + dbg_gates, mark=True)
                       for t in Vroll]
                KR0 = p.op("v", lambda e: e.memset(Kroll[:, :], 0.0),
                           deps=[TCD, GVDMA] + dbg_gates, mark=True)

                kroll_rd = [[KR0], [KR0]]
                kroll_wr = [KR0, KR0]
                vroll_rd = [VR1[0], VR1[1], VR1[2], VR1[3]]
                vroll_wr = [VR1[0], VR1[1], VR1[2], VR1[3]]
                pt_rd = [None] * 3
                comb_rd = None
                dbg_comb = None
                x1war = []
                x1T_rd = None
                x1Tf_rd = None
                sq_prev = None
                p.new_sem("xresem")
                p.new_sem("x1sem")
                p.new_sem("x1tsem0")
                p.new_sem("x1tsem1")

                st_ev = [GKEV, None, None]
                STB = [0, 1, 5]
                loc_ev = [GSIG] * 4
                lsc_ev = GSIG
                lat_ev = [GVEV, XP]
                lpt_ev = GKEV
                pb5_ex = GKEV
                rem_rd = [GQDONE[0], GQDONE[1]]
                wo_rd = [GQDONE[0], GQDONE[1]]
                pb23_rd = [GQDONE[0], GQDONE[1]]
                combT_ev = GVEV
                rtr_rd = [GKEV, GKEV]
                kv_ev = [None, None, GQDONE[0], GQDONE[1]]
                CC = None
                cc_toks = []
                LEXP = None
                pt_dump = None
                dbg_kroll = None
                LATMM_prev = None
                REMMM_prev = None
                COMBDONE_prev = None

                S = {"lsc_ev": GSIG, "lexp": {}, "latmm": None}

                def emit_roll(tc2):
                    if tc2 % 2 != 0:
                        return
                    for b in ([-1, 0] if tc2 == 0 else [tc2 // 2]):
                        slot = b % 2
                        for mc in range(8):
                            bk = mc % 2
                            for kc in range(8):
                                p.op("t", lambda e, mc=mc, kc=kc, b=b, bkk=PB[bk]:
                                     e.matmul(bkk[:, 0:256],
                                              wk_t[:, kc * 1024 + mc * 128: kc * 1024 + mc * 128 + 128],
                                              xT[:, kc * 2304 + 256 + b * 256: kc * 2304 + 256 + b * 256 + 256],
                                              start=(kc == 0), stop=(kc == 7)),
                                     deps=[XTD[0], XTD[1], st_ev[bk], kv_ev[bk]] + WKL)
                            tk = p.mark("t")
                            p.op("v", lambda e, mc=mc, slot=slot, bkk=PB[bk]:
                                 e.tensor_copy(Kroll[0:64, (2 * mc) * 512 + slot * 256:
                                                     (2 * mc) * 512 + slot * 256 + 256],
                                               bkk[0:64, 0:256]),
                                 deps=[tk] + kroll_rd[slot])
                            kv_ev[bk] = p.op(
                                "v", lambda e, mc=mc, slot=slot, bkk=PB[bk]:
                                e.tensor_copy(Kroll[64:128, (2 * mc + 1) * 512 + slot * 256:
                                                    (2 * mc + 1) * 512 + slot * 256 + 256],
                                              bkk[64:128, 0:256]),
                                deps=[tk], mark=True)
                        kroll_wr[slot] = max(kv_ev[0], kv_ev[1], key=lambda t: t[1])
                        for tt in range(2):
                            vt = slot * 2 + tt
                            for nh in range(2):
                                bk = 2 + (tt * 2 + nh) % 2
                                for kc in range(8):
                                    p.op("t", lambda e, tt=tt, nh=nh, kc=kc, b=b, bkk=PB[bk]:
                                         e.matmul(bkk[:, :],
                                                  xT[:, kc * 2304 + 256 + b * 256 + tt * 128:
                                                     kc * 2304 + 256 + b * 256 + tt * 128 + 128],
                                                  wv_t[:, kc * 1024 + nh * 512: kc * 1024 + (nh + 1) * 512],
                                                  start=(kc == 0), stop=(kc == 7)),
                                         deps=[rem_rd[bk - 2], pb23_rd[bk - 2],
                                               kv_ev[bk]] + WVL)
                                tk = p.mark("t")
                                dst = (Vroll[vt][:, :].rearrange("p (h c) -> p h c", c=65)
                                       [:, nh * 8:(nh + 1) * 8, 0:64])
                                src = PB[bk][:, :].rearrange("p (h c) -> p h c", c=64)
                                kv_ev[bk] = p.op("v", lambda e, d=dst, s=src: e.tensor_copy(d, s),
                                                 deps=[tk, vroll_rd[vt]], mark=True)
                            vroll_wr[vt] = max(kv_ev[2], kv_ev[3], key=lambda t: t[1])

                def emit_lsc(tc2):
                    for lh in range(NLH):
                        p.op("t", lambda e, lh=lh, tc=tc2:
                             e.matmul(PB[4][:, lh * 64:(lh + 1) * 64],
                                      GQT[:, (lh // 2) * 2048 + tc * 128:
                                          (lh // 2) * 2048 + tc * 128 + 128],
                                      GKT[:, lh * 64:(lh + 1) * 64],
                                      start=True, stop=True),
                             deps=[GQDONE[0], GQDONE[1], GKEV, S["lsc_ev"]] + loc_ev)
                    LSCMM = p.mark("t")
                    lx = p.op("s", lambda e:
                              e.activation(lp_tok[:, :], PB[4][:, :], AF.Exp, scale=0.125),
                              deps=[LSCMM, S["lexp"].get(tc2 - 1), S["latmm"]], mark=True)
                    S["lexp"][tc2] = lx
                    S["lsc_ev"] = lx

                emit_roll(0)
                emit_lsc(0)
                dbg_kroll = dump("Kroll0", Kroll[:, :], [128, 8192], BF16,
                                 [kroll_wr[0], kroll_wr[1]])

                for tc in range(min(NT, KTC)):
                    n, par = tc // 2, tc % 2
                    cslot = tc % 2
                    LEXP = S["lexp"][tc]
                    lsc_ev = LEXP
                    lm = []
                    for lh in range(NLH):
                        lm.append(p.op("v", lambda e, lh=lh, tc=tc:
                                       e.tensor_tensor(lp_tok[:, lh * 64:(lh + 1) * 64],
                                                       lp_tok[:, lh * 64:(lh + 1) * 64],
                                                       latmask[:, tc * 64:(tc + 1) * 64], ALU.mult),
                                       deps=[LEXP, MASKS], mark=True))
                    sums = []
                    for lh in range(NLH):
                        sums.append(p.op("v", lambda e, lh=lh:
                                         e.reduce_sum(stat[:, 8 + lh:9 + lh],
                                                      lp_tok[:, lh * 64:(lh + 1) * 64], axis=AX),
                                         deps=[lm[lh]], mark=True))
                    if KLAT < 1:
                        continue
                    RCP = p.op("v", lambda e: e.reciprocal(stat[:, 16:24], stat[:, 8:16]),
                               deps=sums, mark=True)
                    nrm = []
                    for lh in range(NLH):
                        nrm.append(p.op("v", lambda e, lh=lh:
                                        e.tensor_scalar(x1Tf[:, lh * 64:(lh + 1) * 64],
                                                        lp_tok[:, lh * 64:(lh + 1) * 64],
                                                        stat[:, 16 + lh:17 + lh], None, ALU.mult),
                                        deps=[RCP, x1Tf_rd, sq_prev], mark=True))
                    if KLAT < 2:
                        continue
                    lpdump = dump("lp_tok0", x1Tf[:, 0:512], [128, 512], F32, nrm) \
                        if tc == 0 else None
                    for j in range(4):
                        p.op("t", lambda e, j=j:
                             e.transpose(PB[5][:, j * 128:(j + 1) * 128],
                                         x1Tf[:, j * 128:(j + 1) * 128], id_f[:, :]),
                             deps=[nrm[2 * j], nrm[2 * j + 1], CONST, lpt_ev, pb5_ex,
                                   rtr_rd[0], rtr_rd[1], lpdump])
                    TTS = p.mark("t")
                    LPTEV = p.op("v", lambda e: e.tensor_copy(lptn[:, :], PB[5][:, :]),
                                 deps=[TTS] + nrm, mark=True)
                    lpt_ev = LPTEV
                    if KLAT < 3:
                        continue
                    for lh in range(NLH):
                        p.op("t", lambda e, lh=lh:
                             e.matmul(PB[6 + lh // 4][0:64, (lh % 4) * 128:(lh % 4) * 128 + 128],
                                      GV2z[:, 64 * lh: 64 * lh + 64],
                                      lptn[:, (lh // 2) * 128:(lh // 2) * 128 + 128],
                                      start=True, stop=True),
                             deps=[LPTEV, GV2D, lat_ev[lh // 4], combT_ev, COMBDONE_prev])
                    LATMM = p.mark("t")
                    LATMM_prev = LATMM
                    S["latmm"] = LATMM
                    LATEV0 = p.op("v", lambda e: e.tensor_copy(latT8[0:64, 0:512], PB[6][0:64, 0:512]),
                                  deps=[LATMM, REMMM_prev], mark=True)
                    LATEV1 = p.op("v", lambda e: e.tensor_copy(latT8[0:64, 512:1024], PB[7][0:64, 0:512]),
                                  deps=[LATMM], mark=True)
                    lat_ev = [LATEV0, LATEV1]
                    if KLAT < 4:
                        continue
                    for lh in range(NLH):
                        for nh2 in range(2):
                            p.op("t", lambda e, lh=lh, nh2=nh2:
                                 e.matmul(PB[2 + nh2][:, :],
                                          latT8[0:64, lh * 128:(lh + 1) * 128],
                                          ro8[0:64, lh * 1024 + nh2 * 512: lh * 1024 + (nh2 + 1) * 512],
                                          start=(lh == 0), stop=(lh == 7)),
                                 deps=[lat_ev[lh // 4], RO, rem_rd[nh2], pb23_rd[nh2], kv_ev[2 + nh2]])
                    REMMM = p.mark("t")
                    REMMM_prev = REMMM
                    if KLAT < 5:
                        continue
                    rg = []
                    for h in range(NH):
                        rg.append(p.op("v", lambda e, h=h, tc=tc:
                                       e.tensor_scalar(comb[:, h * 64:(h + 1) * 64],
                                                       PB[2 + h // 8][:, (h % 8) * 64:(h % 8) * 64 + 64],
                                                       glgr[:, tc * 32 + 16 + h: tc * 32 + 17 + h],
                                                       None, ALU.mult),
                                       deps=[REMMM, GSIG, comb_rd, dbg_comb],
                                       mark=True))
                    rem_rd[:] = [rg[7], rg[15]]
                    if tc == 0:
                        dtk = dump("comb_rem0", comb[:, :], [128, 1024], BF16, rg)
                        if dtk:
                            rg = [max(t, dtk, key=lambda z: (z[0] != "pv", )) if False else t
                                  for t in rg]
                            dbg_comb = dtk
                            rg = [dtk if False else t for t in rg]
                            rg_gate = dtk
                        else:
                            rg_gate = None
                    else:
                        rg_gate = None

                    if tc == 0:
                        dtk = dump("latT80", latT8[0:64, :], [64, 1024], BF16, lat_ev)
                        if dtk:
                            REMMM_prev = dtk

                    if KPART < 2:
                        continue
                    fin_toks = []
                    sm = {}

                    ACT = (0, 1, 2) if par == 0 else (1, 2, 3)
                    cbase = 0 if par == 0 else 128

                    def emit_st(h):
                        i3 = h % 3
                        bkid = STB[i3]
                        extra = []
                        if bkid in (0, 1):
                            extra = [kv_ev[bkid]]
                        else:
                            extra = [lpt_ev, rtr_rd[0], rtr_rd[1]]
                        for kc in ACT:
                            bb = n - 1 + kc // 2
                            kslot = bb % 2
                            p.op("t", lambda e, h=h, kc=kc, kslot=kslot, tc=tc, bkk=PB[bkid]:
                                 e.matmul(bkk[:, kc * 128:(kc + 1) * 128],
                                          Kroll[:, h * 512 + kslot * 256 + (kc % 2) * 128:
                                                h * 512 + kslot * 256 + (kc % 2) * 128 + 128],
                                          QT[:, (h // 2) * 2048 + tc * 128:
                                             (h // 2) * 2048 + tc * 128 + 128],
                                          start=True, stop=True),
                                 deps=[kroll_wr[kslot], QDONE[0], QDONE[1], st_ev[i3]] + extra)
                        return p.mark("t")

                    def emit_sm(h, stmm):
                        nonlocal pb5_ex
                        i3 = h % 3
                        EX = p.op("s", lambda e, i3=i3, cb=cbase:
                                  e.activation(Pt[i3][:, cb:cb + 384],
                                               PB[STB[i3]][:, cb:cb + 384], AF.Exp, scale=0.125),
                                  deps=[stmm, pt_rd[i3], pt_dump if i3 == 0 else None], mark=True)
                        if STB[i3] == 5:
                            pb5_ex = EX
                        var = par + 2 * (1 if n == 0 else 0)
                        MK = p.op("v", lambda e, i3=i3, var=var, cb=cbase:
                                  e.tensor_tensor(Pt[i3][:, cb:cb + 384],
                                                  Pt[i3][:, cb:cb + 384],
                                                  maskT[:, var * 512 + cb: var * 512 + cb + 384],
                                                  ALU.mult),
                                  deps=[EX, MASKS], mark=True)
                        return EX, MK

                    def emit_pv(h):
                        nonlocal pt_dump
                        if tc == 0 and h == 0 and pt_dump is None:
                            pt_dump = dump("Pt00", Pt[0][:, 0:384], [128, 384], BF16, [sm[0][1]])
                            if pt_dump:
                                pt_rd[0] = pt_dump
                        i3 = h % 3
                        i4 = h % 4
                        lb = 4 if h % 2 == 0 else 6
                        col = 128 * ((h // 2) % 2)
                        mk = sm[h][1]
                        for kc in ACT:
                            bb = n - 1 + kc // 2
                            vt = (bb % 2) * 2 + (kc % 2)
                            p.op("t", lambda e, h=h, kc=kc, vt=vt, i3=i3, lb=lb, col=col,
                                 s_=(kc == ACT[0]), t_=(kc == ACT[-1]):
                                 e.matmul(PB[lb][:, col: col + 65],
                                          Pt[i3][:, kc * 128:(kc + 1) * 128],
                                          Vroll[vt][:, h * 65: h * 65 + 65],
                                          start=s_, stop=t_),
                                 deps=[mk, vroll_wr[vt], loc_ev[h % 2], lsc_ev,
                                       lat_ev[0] if lb == 6 else None])
                        tk = p.mark("t")
                        st_ev[i3] = tk
                        pt_rd[i3] = tk
                        return tk

                    def emit_fin(h, pvtk):
                        i4 = h % 4
                        lb = 4 if h % 2 == 0 else 6
                        col = 128 * ((h // 2) % 2)
                        R = p.op("v", lambda e, lb=lb, col=col:
                                 e.reciprocal(stat[:, 0:1], PB[lb][:, col + 64: col + 65]),
                                 deps=[pvtk], mark=True)
                        G = p.op("v", lambda e, h=h, tc=tc:
                                 e.tensor_tensor(stat[:, 1:2], glgr[:, tc * 32 + h: tc * 32 + h + 1],
                                                 stat[:, 0:1], ALU.mult),
                                 deps=[R, GSIG], mark=True)
                        A = p.op("v", lambda e, h=h, lb=lb, col=col:
                                 e.scalar_tensor_tensor(comb[:, h * 64:(h + 1) * 64],
                                                        PB[lb][:, col: col + 64],
                                                        stat[:, 1:2],
                                                        comb[:, h * 64:(h + 1) * 64],
                                                        ALU.mult, ALU.add),
                                 deps=[G, rg[h], rg_gate], mark=True)
                        loc_ev[h % 2] = A
                        fin_toks.append(A)

                    PIPE = int(os.environ.get('KPIPE', '2'))
                    last_pv = None
                    for h in range(NH + PIPE):
                        if h < NH:
                            stm = emit_st(h)
                            sm[h] = (None, None)
                            sm[h] = emit_sm(h, stm)
                        if h >= PIPE:
                            hh = h - PIPE
                            last_pv = emit_pv(hh)
                            emit_fin(hh, last_pv)
                    COMBDONE = fin_toks[-1]
                    COMBDONE_prev = COMBDONE
                    kroll_rd[:] = [[last_pv, dbg_kroll], [last_pv, dbg_kroll]]
                    vroll_rd[:] = [last_pv] * 4

                    if KPART < 3:
                        continue
                    CB0 = p.op("v", lambda e: e.tensor_copy(x1Tf[:, 0:512], comb[:, 0:512]),
                               deps=[COMBDONE, TTS, x1Tf_rd, sq_prev], mark=True)
                    CB1 = p.op("v", lambda e: e.tensor_copy(x1Tf[:, 512:1024], comb[:, 512:1024]),
                               deps=[COMBDONE], mark=True)
                    ctev = []
                    for rnd in range(2):
                        for q4 in range(4):
                            dc = rnd * 4 + q4
                            p.op("t", lambda e, dc=dc, q4=q4:
                                 e.transpose(PB[6][:, q4 * 128:(q4 + 1) * 128],
                                             x1Tf[:, dc * 128:(dc + 1) * 128], id_f[:, :]),
                                 deps=[CB0, CB1, CONST, lat_ev[0], combT_ev])
                        tk = p.mark("t")
                        cte = p.op("v", lambda e, rnd=rnd:
                                   e.tensor_copy(comb[:, rnd * 512:(rnd + 1) * 512],
                                                 PB[6][:, :]),
                                   deps=[tk, dbg_comb], mark=True)
                        ctev.append(cte)
                        combT_ev = cte
                    COMBT_T = tk
                    for nh2 in range(2):
                        for kc in range(8):
                            p.op("t", lambda e, nh2=nh2, kc=kc:
                                 e.matmul(PB[6 + nh2][:, :],
                                          comb[:, kc * 128:(kc + 1) * 128],
                                          wo_t[:, kc * 1024 + nh2 * 512: kc * 1024 + (nh2 + 1) * 512],
                                          start=(kc == 0), stop=(kc == 7)),
                                 deps=[ctev[0], ctev[1], WO, lat_ev[1]])
                    WOMM = p.mark("t")
                    comb_rd = WOMM

                    if KPART < 4:
                        continue
                    XRE = p.dma("y", x1_tc[:, :], x_comb[2 + tc, :, :], "xresem",
                                deps=x1war)
                    ADD0 = p.op("v", lambda e:
                                e.scalar_tensor_tensor(x1_tc[:, 0:512], x1_tc[:, 0:512],
                                                       1.0, PB[6][:, :], ALU.mult, ALU.add,
                                                       accum_out=stat[:, 32:33]),
                                deps=[WOMM, XRE], mark=True)
                    ADD1 = p.op("v", lambda e:
                                e.scalar_tensor_tensor(x1_tc[:, 512:1024], x1_tc[:, 512:1024],
                                                       1.0, PB[7][:, :], ALU.mult, ALU.add,
                                                       accum_out=stat[:, 33:34]),
                                deps=[WOMM, XRE], mark=True)
                    wo_rd[:] = [ADD0, ADD1]
                    combT_ev = ADD1
                    if tc + 1 < min(NT, KTC):
                        emit_roll(tc + 1)
                        emit_lsc(tc + 1)
                    SQ = p.op("s", lambda e:
                              e.activation(x1Tf[:, :], x1_tc[:, :], AF.Square,
                                           accum_out=stat[:, 4:5]),
                              deps=[ADD0, ADD1, sq_prev, x1Tf_rd, COMBT_T], mark=True)
                    sq_prev = SQ
                    NM = p.op("v", lambda e: e.scalar_tensor_tensor(
                                  stat[:, 3:4], stat[:, 32:33], -1.0 / 1024,
                                  stat[:, 33:34].to_broadcast((128, 1)) if False else stat[:, 33:34],
                                  ALU.bypass, ALU.bypass),
                              deps=[ADD0, ADD1], mark=True) if False else None
                    NM1 = p.op("v", lambda e: e.tensor_tensor(stat[:, 34:35], stat[:, 32:33],
                                                              stat[:, 33:34], ALU.add),
                               deps=[ADD0, ADD1], mark=True)
                    NM2 = p.op("v", lambda e: e.tensor_scalar(stat[:, 3:4], stat[:, 34:35],
                                                              -1.0 / 1024, None, ALU.mult),
                               deps=[NM1], mark=True)
                    M2C = p.op("v", lambda e: e.tensor_tensor(stat[:, 35:36], stat[:, 3:4],
                                                              stat[:, 3:4], ALU.mult),
                               deps=[NM2], mark=True)
                    VARC = p.op("v", lambda e: e.tensor_scalar(stat[:, 36:37], stat[:, 4:5],
                                                               1.0 / 1024, None, ALU.mult),
                               deps=[SQ], mark=True)
                    VARF = p.op("v", lambda e: e.tensor_tensor(stat[:, 37:38], stat[:, 36:37],
                                                               stat[:, 35:36], ALU.subtract),
                               deps=[VARC, M2C], mark=True)
                    LNV = p.op("s", lambda e: e.activation(stat[:, 5:6], stat[:, 37:38], AF.Ln,
                                                           bias=eps_c[:, :]),
                               deps=[VARF, EPSC], mark=True)
                    RSD = p.op("s", lambda e: e.activation(stat[:, 6:7], stat[:, 5:6], AF.Exp,
                                                           scale=-0.5),
                               deps=[LNV], mark=True)
                    NMR = p.op("v", lambda e: e.tensor_tensor(stat[:, 7:8], stat[:, 3:4],
                                                              stat[:, 6:7], ALU.mult),
                               deps=[RSD, NM2], mark=True)
                    X1 = p.op("v", lambda e:
                              e.tensor_scalar(x1_tc[:, :], x1_tc[:, :], stat[:, 6:7],
                                              stat[:, 7:8], ALU.mult, ALU.add),
                              deps=[NMR, SQ], mark=True)

                    X1T = []
                    X1TF = []
                    x1t_t = None
                    for rnd in range(2):
                        tb = 2 + rnd
                        for q4 in range(4):
                            dc = rnd * 4 + q4
                            p.op("t", lambda e, dc=dc, q4=q4, tb=tb:
                                 e.transpose(PB[tb][:, q4 * 128:(q4 + 1) * 128],
                                             x1_tc[:, dc * 128:(dc + 1) * 128], id_f[:, :]),
                                 deps=[X1, kv_ev[2], kv_ev[3], rem_rd[0], rem_rd[1]])
                        x1t_t = p.mark("t")
                        EB = p.op("v", lambda e, rnd=rnd, tb=tb:
                                  e.tensor_copy(x1T_tc[:, rnd * 512:(rnd + 1) * 512], PB[tb][:, :]),
                                  deps=[x1t_t, x1T_rd], mark=True)
                        EF = p.op("v", lambda e, rnd=rnd, tb=tb:
                                  e.tensor_copy(x1Tf[:, rnd * 512:(rnd + 1) * 512], PB[tb][:, :]),
                                  deps=[x1t_t, x1Tf_rd, sq_prev], mark=True)
                        X1T.append(EB)
                        X1TF.append(EF)
                    pb23_rd[:] = [X1TF[0], X1TF[1]]

                    if KPART < 5:
                        continue
                    rslot = tc % 2
                    roff = 256 + rslot * 16
                    for kc in range(8):
                        p.op("t", lambda e, kc=kc, roff=roff:
                             e.matmul(PB[5][:, roff:roff + 12],
                                      x1Tf[:, kc * 128:(kc + 1) * 128],
                                      lrw[:, kc * 12:(kc + 1) * 12],
                                      start=(kc == 0), stop=False),
                             deps=X1TF + [CONST, rtr_rd[rslot], pb5_ex])
                    p.op("t", lambda e, roff=roff:
                         e.matmul(PB[5][:, roff:roff + 12], ones_f[:, :], lrb[:, :],
                                  start=False, stop=True),
                         deps=[ONESF, CONST])
                    RMM = p.mark("t")
                    x1Tf_rd = RMM
                    E4 = p.op("s", lambda e, roff=roff:
                              e.activation(lp_pad[:, 0:4], PB[5][:, roff:roff + 4], AF.Exp,
                                           accum_out=stat[:, 24:25]),
                              deps=[RMM, LPP0], mark=True)
                    R4 = p.op("v", lambda e: e.reciprocal(stat[:, 25:26], stat[:, 24:25]),
                              deps=[E4], mark=True)
                    LP4 = p.op("v", lambda e: e.tensor_scalar(lp_pad[:, 0:4], lp_pad[:, 0:4],
                                                              stat[:, 25:26], None, ALU.mult),
                               deps=[R4], mark=True)
                    EW0 = p.op("v", lambda e: e.tensor_tensor(rl_sb[:, 0:4], lp_pad[:, 0:4],
                                                              lp_pad[:, 1:5], ALU.add),
                               deps=[LP4, LPP0], mark=True)
                    EW1 = p.op("v", lambda e: e.tensor_tensor(rl_sb[:, 4:8], lp_pad[:, 0:4],
                                                              lp_pad[:, 1:5], ALU.add),
                               deps=[LP4, LPP0], mark=True)
                    RL = p.op("v", lambda e, roff=roff:
                              e.scalar_tensor_tensor(rl_sb[:, :], rl_sb[:, :], 0.1,
                                                     PB[5][:, roff + 4:roff + 12], ALU.mult, ALU.add),
                              deps=[EW0, EW1, RMM], mark=True)
                    E8 = p.op("s", lambda e: e.activation(p8_sb[:, :], rl_sb[:, :], AF.Exp,
                                                          accum_out=stat[:, 26:27]),
                              deps=[RL], mark=True)
                    R8 = p.op("v", lambda e: e.reciprocal(stat[:, 27:28], stat[:, 26:27]),
                              deps=[E8], mark=True)
                    PP = p.op("v", lambda e: e.tensor_scalar(p8_sb[:, :], p8_sb[:, :],
                                                             stat[:, 27:28], None, ALU.mult),
                              deps=[R8], mark=True)
                    M1 = p.op("v", lambda e: e.reduce_max(stat[:, 28:29], p8_sb[:, :], axis=AX),
                              deps=[PP], mark=True)
                    MRa = p.op("v", lambda e: e.tensor_scalar(pm_sb[:, :], p8_sb[:, :],
                                                              stat[:, 28:29], None, ALU.is_ge),
                               deps=[M1, PP], mark=True)
                    MR = p.op("v", lambda e: e.scalar_tensor_tensor(pm_sb[:, :], pm_sb[:, :],
                                                                    -2.0, p8_sb[:, :],
                                                                    ALU.mult, ALU.add),
                              deps=[MRa], mark=True)
                    M2 = p.op("v", lambda e: e.reduce_max(stat[:, 29:30], pm_sb[:, :], axis=AX),
                              deps=[MR], mark=True)
                    S12 = p.op("v", lambda e: e.tensor_tensor(stat[:, 30:31], stat[:, 28:29],
                                                              stat[:, 29:30], ALU.add),
                               deps=[M2, M1], mark=True)
                    R12 = p.op("v", lambda e: e.reciprocal(stat[:, 31:32], stat[:, 30:31]),
                               deps=[S12], mark=True)
                    GEm = p.op("v", lambda e: e.tensor_scalar(ge_sb[:, :], p8_sb[:, :],
                                                              stat[:, 29:30], None, ALU.is_ge),
                               deps=[M2, PP], mark=True)
                    CC = p.op("v", lambda e, tc=tc:
                              e.scalar_tensor_tensor(c_sb[:, tc * 8:(tc + 1) * 8], p8_sb[:, :],
                                                     stat[:, 31:32], ge_sb[:, :], ALU.mult, ALU.mult),
                              deps=[R12, GEm], mark=True)
                    cc_toks.append(CC)
                    rtr_rd[rslot] = CC

                    SX1 = p.dma("y", x1s[tc, :, :], x1_tc[:, :], "x1sem", deps=[X1])
                    SXT = p.dma("y", x1Ts[:, :, tc * 128:(tc + 1) * 128].rearrange("k p m -> p k m"),
                                x1T_tc[:, :].rearrange("p (k m) -> p k m", k=8),
                                f"x1tsem{tc % 2}", deps=X1T)
                    x1war = [SX1, x1t_t, sq_prev, X1]
                    x1T_rd = SXT

                    if tc == 0:
                        dtk = dump("comb0", comb[:, :], [128, 1024], BF16, [COMBDONE] + rg)
                        if dtk:
                            dbg_comb = dtk
                        dtk = dump("x1_0", x1_tc[:, :], [128, 1024], F32, [X1])
                        if dtk:
                            x1war = x1war + [dtk]
                        dump("c0", c_sb[:, 0:8], [128, 8], F32, [CC])

                X1SALL = ("x1sem", p.cnt["x1sem"])
                X1TALL = ("x1tsem0", p.cnt["x1tsem0"])
                X1TALL1 = ("x1tsem1", p.cnt["x1tsem1"])
                es_e.close()
                FT = ("pt", p.cnt["pt"])
                FV = ("pv", p.cnt["pv"])
                FS = ("ps", p.cnt["ps"])
                FGATE = [FT, FV, FS, X1SALL, X1TALL, X1TALL1]
                if DBG:
                    FGATE.extend(dbg[k] for k in dbg)
                EPB = {
                    0: st_ev[0], 1: st_ev[1], 2: pb23_rd[0], 3: pb23_rd[1],
                    4: ("pv", p.cnt["pv"]), 5: rtr_rd[1],
                    6: wo_rd[0], 7: wo_rd[1],
                }
                CCLAST = CC
                EPB5X = pb5_ex

            if KSTAGE < 3:
                raise _SkipRest()
            with ExitStack() as es_ffn:
                def sbf(name, shape, dt):
                    return es_ffn.enter_context(nc.sbuf_tensor(name, shape, dt))

                acc = sbf("acc", [128, NT * 1024], F32)
                x1T_sb = sbf("x1T_sb", [128, 8 * 2048], BF16)
                hT = sbf("hT", [128, 32 * 1024], BF16)
                es_f1 = ExitStack()
                def sbf1(name, shape, dt):
                    return es_f1.enter_context(nc.sbuf_tensor(name, shape, dt))
                w1b = [sbf1(f"w1b{i}", [128, 1024], BF16) for i in range(4)]
                w2b = [sbf1(f"w2b{i}", [128, 8192], BF16) for i in range(2)]
                b1all = sbf1("b1all", [128, 256], F32)

                p.new_sem("xtsem")
                p.new_sem("xtsem1")
                x1T_v = x1T_sb[:, :].rearrange("p (k m) -> p k m", k=8)
                x1Ts_v = x1Ts.rearrange("k p m -> p k m")
                XTLa = p.dma("y", x1T_v[:, :, 0:1024], x1Ts_v[:, :, 0:1024],
                             "xtsem", deps=FGATE)
                XTLb = p.dma("s", x1T_v[:, :, 1024:2048], x1Ts_v[:, :, 1024:2048],
                             "xtsem1", deps=FGATE)
                XT_TOK = [XTLa, XTLb]
                p.new_sem("b1sem")
                B1L = p.dma("y", b1all[:, :].rearrange("p (e f) -> p e f", e=8),
                            b1_in.rearrange("e p f -> p e f"), "b1sem", deps=FGATE)
                p.new_sem("accsem")
                p.new_sem("accsem1")
                acc_v = acc[:, :].rearrange("p (t c) -> p t c", t=NT)
                x1s_v = x1s.rearrange("t p c -> p t c")
                ACCLa = p.dma("y", acc_v[:, 0:8, :], x1s_v[:, 0:8, :],
                              "accsem", deps=FGATE)
                ACCLb = p.dma("y", acc_v[:, 8:16, :], x1s_v[:, 8:16, :],
                              "accsem1", deps=FGATE)
                ACC_TOK = [ACCLa, ACCLb]
                for i in range(4):
                    p.new_sem(f"w1sem{i}")
                for i in range(2):
                    p.new_sem(f"w2sem{i}")

                w1_war = [None, None, None, None]
                w2_war = [None, None]
                w1_tok = {}
                w2_tok = {}

                seq_w1 = [(e_, Tt, f) for e_ in range(NE) for Tt in range(2) for f in range(32)]
                seq_w2 = [(e_, Tt, dq) for e_ in range(NE) for Tt in range(2) for dq in range(4)]

                def load_w1(idx):
                    e_, Tt, f = seq_w1[idx]
                    sl = idx % 4
                    tok = p.dma("g", w1b[sl][:, :], w1_in[e_, f, :, :], f"w1sem{sl}",
                                deps=[w1_war[sl]] + (FGATE if idx < 4 else []))
                    w1_tok[idx] = (tok, sl)

                def load_w2(idx):
                    e_, Tt, dq = seq_w2[idx]
                    sl = idx % 2
                    tok = None
                    for c4 in range(4):
                        tok = p.dma("g", w2b[sl][:, c4 * 2048:(c4 + 1) * 2048],
                                    w2_in[e_, dq, :, c4 * 2048:(c4 + 1) * 2048], f"w2sem{sl}",
                                    deps=([w2_war[sl]] if c4 == 0 else []) +
                                         (FGATE if idx < 2 and c4 == 0 else []))
                    w2_tok[idx] = (tok, sl)

                load_w1(0)
                load_w1(1)
                load_w1(2)
                load_w1(3)
                load_w2(0)
                load_w2(1)

                ct_ev = None
                for rnd in range(4):
                    cb = 6 + rnd % 2
                    for j in range(4):
                        tcg = rnd * 4 + j
                        p.op("t", lambda e, tcg=tcg, j=j, cb=cb:
                             e.transpose(PB[cb][0:8, j * 128:(j + 1) * 128],
                                         c_sb[:, tcg * 8:(tcg + 1) * 8], id_f[:, :]),
                             deps=[cc_toks[tcg], EPB[6] if cb == 6 else EPB[7], ct_ev])
                    tkc = p.mark("t")
                    ct_ev = p.op("v", lambda e, rnd=rnd, cb=cb:
                                 e.tensor_copy(cT[0:8, rnd * 512:(rnd + 1) * 512],
                                               PB[cb][0:8, 0:512]),
                                 deps=[tkc], mark=True)
                CT_LAST = ct_ev

                o_ev = [EPB[4], EPB[5]]
                h_rd = [EPB[0], EPB[1], EPB[2], EPB[3]]
                MM2LAST = None
                iw1 = 0
                iw2 = 0

                for e_ in range(NE):
                    for Tt in range(2):
                        HDONE = None
                        for f in range(32):
                            tokw, sl = w1_tok[iw1]
                            tkmm = None
                            for nh in range(2):
                                bk = 2 * (f % 2) + nh
                                for kc in range(8):
                                    p.op("t", lambda e, f=f, nh=nh, kc=kc, sl=sl, Tt=Tt, bk=bk:
                                         e.matmul(PB[bk][:, :],
                                                  w1b[sl][:, kc * 128:(kc + 1) * 128],
                                                  x1T_sb[:, kc * 2048 + Tt * 1024 + nh * 512:
                                                         kc * 2048 + Tt * 1024 + (nh + 1) * 512],
                                                  start=(kc == 0), stop=(kc == 7)),
                                         deps=[tokw, XT_TOK[Tt], h_rd[bk], MM2LAST])
                                tkmm = p.mark("t")
                                GL = p.op("s", lambda e, f=f, nh=nh, bk=bk:
                                          e.activation(hT[:, f * 1024 + nh * 512:
                                                          f * 1024 + (nh + 1) * 512],
                                                       PB[bk][:, :], AF.Gelu,
                                                       bias=b1all[:, e_ * 32 + f: e_ * 32 + f + 1]),
                                          deps=[tkmm, B1L, MM2LAST], mark=True)
                                h_rd[bk] = GL
                                HDONE = GL
                            w1_war[sl] = tkmm
                            if iw1 + 4 < len(seq_w1):
                                load_w1(iw1 + 4)
                            iw1 += 1
                        for dq in range(4):
                            tokw2, sl2 = w2_tok[iw2]
                            for tcl in range(8):
                                ob = 4 + (tcl % 2)
                                gtc = Tt * 8 + tcl
                                for f in range(32):
                                    p.op("t", lambda e, f=f, tcl=tcl, sl2=sl2, ob=ob:
                                         e.matmul(PB[ob][:, 0:256],
                                                  hT[:, f * 1024 + tcl * 128:
                                                     f * 1024 + tcl * 128 + 128],
                                                  w2b[sl2][:, f * 256:(f + 1) * 256],
                                                  start=(f == 0), stop=(f == 31)),
                                         deps=[tokw2, HDONE, o_ev[tcl % 2],
                                               EPB5X if ob == 5 else None])
                                tk2 = p.mark("t")
                                o_ev[tcl % 2] = p.op(
                                    "v", lambda e, gtc=gtc, dq=dq, e_=e_, ob=ob:
                                    e.scalar_tensor_tensor(
                                        acc[:, gtc * 1024 + dq * 256: gtc * 1024 + (dq + 1) * 256],
                                        PB[ob][:, 0:256],
                                        c_sb[:, gtc * 8 + e_: gtc * 8 + e_ + 1],
                                        acc[:, gtc * 1024 + dq * 256: gtc * 1024 + (dq + 1) * 256],
                                        ALU.mult, ALU.add),
                                    deps=[tk2, ACC_TOK[Tt], CCLAST], mark=True)
                            w2_war[sl2] = tk2
                            if iw2 + 2 < len(seq_w2):
                                load_w2(iw2 + 2)
                            iw2 += 1
                        MM2LAST = tk2

                ACCDONE = max(o_ev[0], o_ev[1], key=lambda t: t[1])
                es_f1.close()
                b2_sb = sbf("b2_sb", [8, 1024], BF16)
                p.new_sem("b2sem")
                B2L = p.dma("g", b2_sb[:, :], b2_in, "b2sem",
                            deps=[MM2LAST, ACCDONE, h_rd[0], h_rd[1], h_rd[2], h_rd[3]])

                p.new_sem("osem0")
                p.new_sem("osem1")
                b2done = None
                y2_war = [None]
                for gtc in range(NT):
                    for nh2 in range(2):
                        p.op("t", lambda e, gtc=gtc, nh2=nh2:
                             e.matmul(PB[6 + nh2][:, :],
                                      cT[0:8, gtc * 128:(gtc + 1) * 128],
                                      b2_sb[0:8, nh2 * 512:(nh2 + 1) * 512],
                                      start=True, stop=True),
                             deps=[B2L, CT_LAST, EPB[6], EPB[7], b2done])
                    BTK = p.mark("t")
                    A0 = p.op("v", lambda e, gtc=gtc:
                              e.tensor_tensor(acc[:, gtc * 1024: gtc * 1024 + 512],
                                              acc[:, gtc * 1024: gtc * 1024 + 512],
                                              PB[6][:, :], ALU.add),
                              deps=[BTK, o_ev[0], o_ev[1], ACCDONE], mark=True)
                    A1 = p.op("v", lambda e, gtc=gtc:
                              e.tensor_tensor(acc[:, gtc * 1024 + 512: gtc * 1024 + 1024],
                                              acc[:, gtc * 1024 + 512: gtc * 1024 + 1024],
                                              PB[7][:, :], ALU.add),
                              deps=[BTK], mark=True)
                    b2done = A1
                    NS = p.op("v", lambda e, gtc=gtc:
                              e.reduce_sum(stat[:, 2:3], acc[:, gtc * 1024:(gtc + 1) * 1024],
                                           axis=AX, negate=True),
                              deps=[A0, A1], mark=True)
                    NM = p.op("v", lambda e: e.tensor_scalar(stat[:, 3:4], stat[:, 2:3],
                                                             1.0 / 1024, None, ALU.mult),
                              deps=[NS], mark=True)
                    SQ = p.op("s", lambda e, gtc=gtc:
                              e.activation(hT[:, 0:1024], acc[:, gtc * 1024:(gtc + 1) * 1024],
                                           AF.Square, bias=stat[:, 3:4], accum_out=stat[:, 4:5]),
                              deps=[A0, A1, NM, y2_war[0], MM2LAST], mark=True)
                    y2_war[0] = SQ
                    LNV = p.op("s", lambda e: e.activation(stat[:, 5:6], stat[:, 4:5], AF.Ln,
                                                           scale=1.0 / 1024, bias=eps_c[:, :]),
                               deps=[SQ, EPSC], mark=True)
                    RSD = p.op("s", lambda e: e.activation(stat[:, 6:7], stat[:, 5:6], AF.Exp,
                                                           scale=-0.5),
                               deps=[LNV], mark=True)
                    NMR = p.op("v", lambda e: e.tensor_tensor(stat[:, 7:8], stat[:, 3:4],
                                                              stat[:, 6:7], ALU.mult),
                               deps=[RSD, NM], mark=True)
                    OA = p.op("v", lambda e, gtc=gtc:
                              e.tensor_scalar(acc[:, gtc * 1024:(gtc + 1) * 1024],
                                              acc[:, gtc * 1024:(gtc + 1) * 1024],
                                              stat[:, 6:7], stat[:, 7:8], ALU.mult, ALU.add),
                              deps=[NMR, SQ], mark=True)
                    ST = p.dma("y", out_ap[gtc * 128:(gtc + 1) * 128, :],
                               acc[:, gtc * 1024:(gtc + 1) * 1024],
                               f"osem{gtc % 2}", deps=[OA])

                p.wait("y", ("osem0", p.cnt["osem0"]))
                p.wait("y", ("osem1", p.cnt["osem1"]))
                if DBG:
                    for k in dbg:
                        p.wait("y", dbg[k])

        except _SkipRest:
            pass

        with nc.Block() as block:
            p.realize(block)

    return nc, sorted(dbg.keys())


class _SkipRest(Exception):
    pass



def _build_masks():
    m = np.zeros((128, 4 * 512), dtype=BF16NP)
    for var in range(4):
        par, fb = var % 2, var // 2
        kp = np.arange(128)[:, None]
        col = np.arange(512)[None, :]
        kc, q = col // 128, col % 128
        j = kc * 128 + kp
        i = par * 128 + q
        ok = (j > i) & (j <= i + 256)
        if fb:
            ok = ok & (j >= 256)
        m[:, var * 512:(var + 1) * 512] = ok.astype(BF16NP)
    return m


def _build_latmask(half):
    m = np.zeros((128, NT * 64), dtype=BF16NP)
    for tc in range(NT):
        pcol = np.arange(128)[:, None]
        n_rel = np.arange(64)[None, :]
        s_glob = half * 2048 + tc * 128 + pcol
        n_g = np.where(n_rel < 32, half * 32 + n_rel, (1 - half) * 32 + (n_rel - 32))
        ok = n_g <= (s_glob // 64)
        m[:, tc * 64:(tc + 1) * 64] = ok.astype(BF16NP)
    return m


def kernel(L0_attn_x, L0_attn_wq, L0_attn_wk, L0_attn_wv, L0_attn_wgq, L0_attn_wgk,
           L0_attn_wgv, L0_attn_r_out_w, L0_attn_gate_local_w, L0_attn_gate_remote_w,
           L0_attn_w_out, L0_ffn_router_w, L0_ffn_router_b, L0_ffn_lang_w, L0_ffn_lang_b,
           L0_ffn_w1, L0_ffn_b1, L0_ffn_w2, L0_ffn_b2):
    f32 = lambda a: np.ascontiguousarray(np.asarray(a, dtype=np.float32))
    x = f32(L0_attn_x)
    wq = f32(L0_attn_wq); wk = f32(L0_attn_wk); wv = f32(L0_attn_wv)
    wgq = f32(L0_attn_wgq); wgk = f32(L0_attn_wgk); wgv = f32(L0_attn_wgv)
    ro = f32(L0_attn_r_out_w); glw = f32(L0_attn_gate_local_w)
    grw = f32(L0_attn_gate_remote_w); wo = f32(L0_attn_w_out)
    rw = f32(L0_ffn_router_w); rb = f32(L0_ffn_router_b)
    lw = f32(L0_ffn_lang_w); lb = f32(L0_ffn_lang_b)
    w1 = f32(L0_ffn_w1); b1 = f32(L0_ffn_b1); w2 = f32(L0_ffn_w2); b2 = f32(L0_ffn_b2)

    maskT_all = _build_masks()
    maskT_h1 = maskT_all.copy()
    maskT_h1[:, 2 * 512:4 * 512] = maskT_all[:, 0:2 * 512]
    shared = {
        "ident": np.eye(128, dtype=np.float32),
        "poolmat": np.kron(np.eye(2, dtype=np.float32),
                           np.full((64, 1), 1.0 / 64, np.float32)),

        "glw": glw, "grw": grw,
        "lrw": np.ascontiguousarray(
            np.concatenate([lw.T, rw.T], axis=1).reshape(8, 128, 12)),
        "lrb": np.concatenate([lb, rb])[None, :].astype(np.float32),
        "wq": np.ascontiguousarray(wq.reshape(8, 128, 8, 128).transpose(1, 0, 2, 3).reshape(128, 8192)),
        "wk": np.ascontiguousarray(wk.reshape(8, 128, 8, 128).transpose(1, 0, 2, 3).reshape(128, 8192)),
        "wo": np.ascontiguousarray(wo.reshape(8, 128, 8, 128).transpose(1, 0, 2, 3).reshape(128, 8192)),
        "wv": np.ascontiguousarray(wv.reshape(8, 128, 1024).transpose(1, 0, 2).reshape(128, 8192)),
        "wgq": np.ascontiguousarray(wgq.reshape(8, 128, 4, 128).transpose(1, 0, 2, 3).reshape(128, 4096)),
        "wgk": np.ascontiguousarray(wgk.reshape(8, 128, 4, 128).transpose(1, 0, 2, 3).reshape(128, 4096)),
        "wgv": np.ascontiguousarray(wgv.reshape(8, 128, 512).transpose(1, 0, 2).reshape(128, 4096)),
        "ro": np.ascontiguousarray(ro.reshape(8, 64, 1024).transpose(1, 0, 2).reshape(64, 8192)),
        "b1": np.ascontiguousarray(b1.reshape(8, 32, 128).transpose(0, 2, 1)),
        "b2": b2,
        "w1": np.ascontiguousarray(
            w1.reshape(8, 8, 128, 32, 128).transpose(0, 3, 2, 1, 4).reshape(8, 32, 128, 1024)),
        "w2": np.ascontiguousarray(
            w2.reshape(8, 32, 128, 4, 256).transpose(0, 3, 2, 1, 4).reshape(8, 4, 128, 8192)),
    }
    latmasks = [_build_latmask(0), _build_latmask(1)]

    in_maps = []
    for c in range(8):
        b, half = c // 2, c % 2
        own = x[b, half * 2048:(half + 1) * 2048]
        halo = (x[b, 2048 - 256:2048] if half == 1 else np.zeros((256, D), np.float32))
        other = x[b, (1 - half) * 2048:(2 - half) * 2048]
        x_cb = np.ascontiguousarray(
            np.concatenate([halo, own, other], axis=0).reshape(34, 128, D))
        m = dict(shared)
        m["x_comb"] = x_cb
        m["latmask"] = latmasks[half]
        m["maskT"] = maskT_all if half == 0 else maskT_h1
        in_maps.append(m)

    nc, dbg_names = build_program()
    res = run_bass_kernel_spmd(nc, in_maps, core_ids=list(range(8)))

    out = np.empty((4, 4096, D), dtype=np.float32)
    for c in range(8):
        b, half = c // 2, c % 2
        out[b, half * 2048:(half + 1) * 2048] = res.results[c]["out"]
    if DBG:
        kernel._dbg = [{k: r.get("dbg_" + k) for k in dbg_names} for r in res.results]
    return out


if __name__ == "__main__":
    nc, names = build_program()
    print("built ok; dbg:", names)
